# version 03
# baseline (speedup 1.0000x reference)
import contextlib
import numpy as np
import ml_dtypes
import concourse.bass as bass
import concourse.mybir as mybir
from concourse.bass_utils import run_bass_kernel_spmd

F32 = mybir.dt.float32
F16 = mybir.dt.float16
I16 = mybir.dt.int16
U8 = mybir.dt.uint8
AF = mybir.ActivationFunctionType
OP = mybir.AluOpType

P = 128
B_, L, DM = 4, 4096, 2048
TOK = 2048
CH = 512
NCH = 4
SC = 8
KT = 16
MLP = 16384
NST = 16
DTR = 128
DTMY = 8
NG_A = 32
NG_B = 16
EPS = 1e-5
SCL14 = 16384.0
RG = [[0, 1], [2, 3], [4, 5], [6, 7]]
DEBUG = False



def _stage(inp):
    ev = np.arange(0, DM, 2)
    od = np.arange(1, DM, 2)
    perms = [np.concatenate([ev, od]), np.concatenate([od, ev])]
    pN = np.concatenate([np.arange(0, MLP, 2), np.arange(1, MLP, 2)])

    x = np.asarray(inp["L0_attn_x"], np.float32)
    w_in = np.asarray(inp["L0_attn_W_in"], np.float32)
    conv_w = np.asarray(inp["L0_attn_conv_w"], np.float32)
    conv_b = np.asarray(inp["L0_attn_conv_b"], np.float32)
    w_x = np.asarray(inp["L0_attn_W_x"], np.float32)
    w_dt = np.asarray(inp["L0_attn_W_dt"], np.float32)
    b_dt = np.asarray(inp["L0_attn_b_dt"], np.float32)
    a_log = np.asarray(inp["L0_attn_A_log"], np.float32)
    dvec_ = np.asarray(inp["L0_attn_D"], np.float32)
    w_out = np.asarray(inp["L0_attn_W_out"], np.float32)

    oh = np.zeros((32, 32 * P), np.float32)
    for n in range(32):
        oh[n, n * P:(n + 1) * P] = 1.0
    e8 = np.zeros((8, P), np.float32)
    for j in range(8):
        e8[j, 16 * j:16 * (j + 1)] = SCL14

    shared = {}
    for par in (0, 1):
        pm = perms[par]
        cp = np.concatenate([pm, DM + pm])
        d = {}
        d["w_in_t"] = np.ascontiguousarray(
            w_in[pm][:, cp].reshape(KT, P, 32, P).transpose(2, 1, 0, 3))
        d["convw_t"] = np.ascontiguousarray(
            conv_w[pm][:, pm, :].transpose(2, 1, 0)
            .reshape(4, KT, P, KT, P).transpose(0, 3, 2, 1, 4))
        d["wx_t"] = np.ascontiguousarray(
            w_x[pm].reshape(KT, P, 160).transpose(1, 0, 2))
        d["wdt_t"] = np.ascontiguousarray(w_dt[:, pm[:1024]])
        d["wout_t"] = np.ascontiguousarray(
            w_out[pm][:, pm].reshape(KT, P, KT, P).transpose(2, 1, 0, 3))
        d["cvec_t"] = np.ascontiguousarray(conv_b[pm].reshape(KT, P).T)
        d["dvec_t"] = np.ascontiguousarray(dvec_[pm].reshape(KT, P).T)
        d["bdt_t"] = np.ascontiguousarray(b_dt[pm].reshape(KT, P).T)
        d["alog_t"] = np.ascontiguousarray(
            a_log[pm[:1024]].reshape(DTMY, P, NST).transpose(1, 0, 2)
            .reshape(P, DTMY * NST))
        for nm in ("gate", "up"):
            w = (np.asarray(inp[f"L0_ffn_{nm}_W_packed"]) & 0xFF
                 ).astype(np.uint8)[pN]
            if par == 1:
                w = (((w >> 4) | (w << 4)) & 0xFF).astype(np.uint8)
            wt = np.ascontiguousarray(w.T)
            d[f"{nm}_pk"] = np.ascontiguousarray(
                wt.reshape(8, P, NG_A, CH).transpose(2, 1, 0, 3))
            sc = np.asarray(inp[f"L0_ffn_{nm}_scales"], np.float32)[pN]
            d[f"{nm}_sc"] = np.ascontiguousarray(sc.T)
        wdp = (np.asarray(inp["L0_ffn_down_W_packed"]) & 0xFF
               ).astype(np.uint8)[pm]
        d["down_pk"] = np.ascontiguousarray(wdp.T.reshape(64, P, DM))
        d["down_sc"] = np.ascontiguousarray(
            np.asarray(inp["L0_ffn_down_scales"], np.float32)[pm].T)
        d["e8_t"] = e8
        d["oh_t"] = oh
        shared[par] = d

    in_maps = []
    for c in range(8):
        b, par = c // 2, c % 2
        pm = perms[par]
        m = dict(shared[par])
        xt = x[b, par * TOK:(par + 1) * TOK][:, pm]
        m["x_t"] = np.ascontiguousarray(xt.T)
        halo = np.zeros((DM, 4), np.float32)
        if par == 1:
            halo[:, 1:4] = x[b, TOK - 3:TOK][:, pm].T
        m["xhalo_t"] = np.ascontiguousarray(
            halo.reshape(KT, P, 4).transpose(1, 0, 2).reshape(P, KT * 4))
        in_maps.append(m)
    return in_maps, perms



class Plan:
    ENG = ("sync", "vector", "scalar", "tensor", "gpsimd")

    def __init__(self, nc):
        self.nc = nc
        self.items = {e: [] for e in self.ENG}
        self.cnt = {}
        self.h = {}
        self.bank_rel = [0] * 8
        self.trace = []
        self.wvals = []
        self.tagno = 0

    def rec(self, kind, e, sem, v):
        self.tagno += 1
        self.trace.append((kind, e, sem, v, self.tagno))

    def simulate(self, raise_=True):
        per = {e: [t for t in self.trace if t[1] == e] for e in self.ENG}
        ptr = {e: 0 for e in self.ENG}
        cnt = {s: 0 for s in self.cnt}
        progress = True
        while progress:
            progress = False
            for e in self.ENG:
                while ptr[e] < len(per[e]):
                    kind, _, sem, v, tag = per[e][ptr[e]]
                    if kind == "I":
                        cnt[sem] += v
                        ptr[e] += 1
                        progress = True
                    elif cnt[sem] >= self.wvals[v]:
                        ptr[e] += 1
                        progress = True
                    else:
                        break
        stuck = [per[e][ptr[e]] for e in self.ENG if ptr[e] < len(per[e])]
        if stuck and raise_:
            for t in stuck:
                print(f"DEADLOCK {t[1]}: waiting {t[2]} >= "
                      f"{self.wvals[t[3]]} (have {cnt[t[2]]}) tag={t[4]}")
            raise RuntimeError("plan deadlock")
        return stuck

    def lag_waits(self):
        import bisect
        pos = {s: [] for s in self.cnt}
        cum = {s: 0 for s in self.cnt}
        for t in self.trace:
            if t[0] == "I":
                cum[t[2]] += t[3]
                pos[t[2]].append(cum[t[2]])
        lagged = {}
        for t in self.trace:
            if t[0] != "W" or t[2] not in ("ac", "dv", "evg"):
                continue
            idx = t[3]
            v = self.wvals[idx]
            lst = pos[t[2]]
            j = bisect.bisect_left(lst, v)
            if j + 1 < len(lst):
                lagged[idx] = v
                self.wvals[idx] = lst[j + 1]
        per = {e: [t for t in self.trace if t[1] == e] for e in self.ENG}
        ptr = {e: 0 for e in self.ENG}
        cnt = {s: 0 for s in self.cnt}
        nrev = 0
        while True:
            progress = True
            while progress:
                progress = False
                for e in self.ENG:
                    while ptr[e] < len(per[e]):
                        kind, _, sem, v, tag = per[e][ptr[e]]
                        if kind == "I":
                            cnt[sem] += v
                            ptr[e] += 1
                            progress = True
                        elif cnt[sem] >= self.wvals[v]:
                            ptr[e] += 1
                            progress = True
                        else:
                            break
            stuck = [per[e][ptr[e]] for e in self.ENG if ptr[e] < len(per[e])]
            if not stuck:
                break
            hit = False
            for t in stuck:
                if t[0] == "W" and t[3] in lagged:
                    self.wvals[t[3]] = lagged.pop(t[3])
                    nrev += 1
                    hit = True
            if not hit:
                for t in stuck:
                    print(f"DEADLOCK {t[1]}: {t[2]} >= {self.wvals[t[3]]} "
                          f"(have {cnt[t[2]]}) tag={t[4]}")
                raise RuntimeError("plan deadlock (unlagged)")
        print(f"lag_waits: {len(lagged)} lagged, {nrev} reverted")

    def sem(self, name, handle):
        self.cnt[name] = 0
        self.h[name] = handle

    def inc(self, name, v=1):
        self.cnt[name] += v
        return self.cnt[name]

    def val(self, name):
        return self.cnt[name]

    def emit(self, e, fn):
        self.items[e].append(fn)

    def wait(self, e, name, v):
        if v <= 0:
            return
        assert v <= self.cnt[name], (e, name, v, self.cnt[name])
        idx = len(self.wvals)
        self.wvals.append(v)
        self.rec("W", e, name, idx)
        h = self.h[name]
        self.emit(e, lambda eng, h=h, i=idx: eng.wait_ge(h, self.wvals[i]))

    def op(self, e, fn, sem=None, v=1):
        if sem is None:
            self.emit(e, lambda eng, fn=fn: fn(eng))
            return None
        cnt = self.inc(sem, v)
        self.rec("I", e, sem, v)
        h = self.h[sem]

        def go(eng, fn=fn, h=h, v=v):
            fn(eng).then_inc(h, v)
        self.emit(e, go)
        return cnt

    def dma(self, e, name, out, in_):
        cnt = self.inc(name, 16)
        self.rec("I", e, name, 16)
        h = self.h[name]
        self.emit(e, lambda eng, h=h, o=out, i=in_:
                  eng.dma_start(out=o, in_=i).then_inc(h, 16))
        return cnt

    def _conds(self, eng):
        key = eng.engine
        if not hasattr(self, "_cond_cache"):
            self._cond_cache = {}
        if key not in self._cond_cache:
            pid = self.nc.partition_id(engines=[key])
            par = self.nc.snap(pid % 2)
            self._cond_cache[key] = (self.nc.snap(par == 0),
                                     self.nc.snap(par == 1))
        return self._cond_cache[key]

    def dma_pair(self, e, name, pairs):
        cnt = self.inc(name, 32 * len(pairs))
        self.rec("I", e, name, 32 * len(pairs))
        h = self.h[name]

        def go(eng, pairs=pairs, h=h):
            c_ev, c_od = self._conds(eng)
            for oe, ie, oo, io in pairs:
                eng.dma_start(out=oe, in_=ie, cond=c_ev).then_inc(h, 16)
                eng.dma_start(out=oo, in_=io, cond=c_od).then_inc(h, 16)
        self.emit(e, go)
        return cnt

    def pe_use(self, *banks):
        v = max(self.bank_rel[b] for b in banks)
        if v > 0:
            self.wait("tensor", "evg", v)

    def bank_done(self, ev, *banks):
        for b in banks:
            self.bank_rel[b] = ev

    def evac(self, e, fn, banks, extra_sem=None):
        ev = self.op(e, fn, "evg")
        self.bank_done(ev, *banks)
        if extra_sem is not None:
            return ev, None
        return ev



def build_program():
    nc = bass.Bass("TRN2", target_bir_lowering=False)

    def din(name, shape, dt=F32):
        return nc.dram_tensor(name, shape, dt, kind="ExternalInput").ap()

    x_t = din("x_t", [DM, TOK])
    xhalo_t = din("xhalo_t", [P, KT * 4])
    w_in_t = din("w_in_t", [32, P, KT, P])
    convw_t = din("convw_t", [4, KT, P, KT, P])
    wx_t = din("wx_t", [P, KT, 160])
    wdt_t = din("wdt_t", [DTR, 1024])
    wout_t = din("wout_t", [KT, P, KT, P])
    cvec_t = din("cvec_t", [P, KT])
    dvec_t = din("dvec_t", [P, KT])
    bdt_t = din("bdt_t", [P, KT])
    alog_t = din("alog_t", [P, DTMY * NST])
    gate_pk = din("gate_pk", [NG_A, P, 8, CH], U8)
    up_pk = din("up_pk", [NG_A, P, 8, CH], U8)
    down_pk = din("down_pk", [64, P, DM], U8)
    gate_sc = din("gate_sc", [64, MLP])
    up_sc = din("up_sc", [64, MLP])
    down_sc = din("down_sc", [512, DM])
    e8_t = din("e8_t", [8, P])
    oh_t = din("oh_t", [32, 32 * P])

    out_t = nc.dram_tensor("out_t", [DM, TOK], F32, kind="ExternalOutput").ap()

    dkind = dict(kind="ExternalOutput") if DEBUG else {}
    xs_d = nc.dram_tensor("xs_d", [DM, TOK], F32).ap()
    r_d = nc.dram_tensor("r_d", [DM, TOK], F16, **dkind).ap()
    int_d = nc.dram_tensor("int_d", [MLP, TOK], F16, **dkind).ap()
    g_d = nc.dram_tensor("g_d", [DM, TOK], F16, **dkind).ap()
    y_d = nc.dram_tensor("y_d", [1024, L], F32).ap()
    xsc_d = nc.dram_tensor("xsc_d", [1024, L], F32, **dkind).ap()
    cc1i = nc.dram_tensor("cc1i", [160, TOK], F32).ap()
    cc1o = nc.dram_tensor("cc1o", [160, TOK], F32).ap()
    cc2i = nc.dram_tensor("cc2i", [1024, TOK], F32).ap()
    cc2o = nc.dram_tensor("cc2o", [1024, TOK], F32).ap()
    cc3o = nc.dram_tensor("cc3o", [1024, L], F32).ap()
    dbg_d = nc.dram_tensor("dbg_d", [DTR, L], F32).ap()
    bcg_d = nc.dram_tensor("bcg_d", [32, L], F32).ap()
    pxs_d = nc.dram_tensor("pxs_d", [1024, TOK], F32).ap()
    ymg_d = nc.dram_tensor("ymg_d", [DM, TOK], F32).ap()

    es = contextlib.ExitStack()
    sb = lambda n, shp, dt: es.enter_context(nc.sbuf_tensor(n, shp, dt))

    pl = Plan(nc)
    for s in ["lx", "lw", "lp", "ls", "li", "le", "lc",
              "peg", "evg", "dv", "ac", "po",
              "sx", "sr", "si", "sy", "sm", "so", "cc"]:
        pl.sem(s, es.enter_context(nc.semaphore(s)))

    bank = [es.enter_context(nc.psum_tensor(f"pb{i}", [P, CH], F32))
            for i in range(8)]

    ones_col = sb("ones_col", [P, 1], F16)
    ones_row = sb("ones_row", [1, P], F16)
    e8_sb = sb("e8_sb", [8, P], F16)
    cvec = sb("cvec", [P, KT], F32)
    dvec = sb("dvec", [P, KT], F32)
    bdt = sb("bdt", [P, KT], F32)
    aneg = sb("aneg", [P, DTMY * NST], F32)
    wdt16 = sb("wdt16", [DTR, 1024], F16)
    carry = sb("carry", [P, NST * DTMY], F32)
    strow = sb("strow", [1, 3 * CH], F32)
    strow16 = sb("strow16", [1, 2 * CH], F16)
    drsrc = sb("drsrc", [P, P], F16)

    pl.op("vector", lambda v: v.memset(ones_col[:, :], 1.0))
    pl.op("vector", lambda v: v.memset(ones_row[:, :], 1.0))
    pl.op("vector", lambda v: v.memset(drsrc[:, :], 0.0))
    d_ones = pl.op("vector", lambda v: v.memset(carry[:, :], 0.0), "dv")
    pl.op("vector", lambda v: v.tensor_tensor_scan(
        strow[0:1, 2 * CH:2 * CH + 16], strow[0:1, 0:16], strow[0:1, 16:32],
        0.0, OP.mult, OP.add))
    pl.op("vector", lambda v: v.scalar_tensor_tensor(
        strow[0:1, 2 * CH:2 * CH + 16], strow[0:1, 0:16], 1.0,
        strow[0:1, 16:32], OP.mult, OP.add))
    pl.op("vector", lambda v: v.tensor_tensor_scan(
        strow[0:1, 2 * CH:2 * CH + 16], strow[0:1, 0:16], strow[0:1, 16:32],
        0.0, OP.mult, OP.add))

    def fence_act(sem="ac", v=1):
        return pl.op("scalar", lambda a: a.activation(
            strow[0:1, 8:9], strow[0:1, 9:10], AF.Copy), sem, v)

    def fence_dv(sem="dv", v=1):
        return pl.op("vector", lambda vv: vv.tensor_copy(
            strow[0:1, 2 * CH:2 * CH + 256], strow[0:1, CH:CH + 256]), sem, v)

    def drain_mm(t):
        return t.matmul(bank[5][0:1, 0:P], ones_col[:, :1], drsrc[:, :],
                        start=True, stop=True)
    l_e8 = pl.dma("gpsimd", "lw", e8_sb[:, :], e8_t)
    pl.dma("sync", "lc", cvec[:, :], cvec_t)
    pl.dma("sync", "lc", dvec[:, :], dvec_t)
    pl.dma("sync", "lc", bdt[:, :], bdt_t)
    pl.dma("sync", "lc", aneg[:, :], alog_t)
    l_wdt = pl.dma("gpsimd", "lw", wdt16[:, :], wdt_t)
    pl.wait("scalar", "lc", pl.val("lc"))
    a_exp = pl.op("scalar", lambda a: a.activation(
        aneg[:, :], aneg[:, :], AF.Exp), "ac")
    pl.wait("vector", "ac", a_exp)
    pl.wait("vector", "lc", pl.val("lc"))
    d_aneg = pl.op("vector", lambda v: v.tensor_scalar(
        aneg[:, :], aneg[:, :], -1.0, None, OP.mult), "dv")

    es_xn = contextlib.ExitStack()
    xn = es_xn.enter_context(nc.sbuf_tensor("xn", [P, KT * TOK], F16))

    es_p0 = contextlib.ExitStack()
    sb0 = lambda n, shp, dt: es_p0.enter_context(nc.sbuf_tensor(n, shp, dt))
    xh = sb0("xh", [P, KT * CH], F16)
    xsq = sb0("xsq", [P, KT * CH], F16)
    xnh = sb0("xnh", [P, KT * 4], F16)
    lnsc = sb0("lnsc", [P, CH], F32)

    x_r = x_t.rearrange("(t p) n -> t p n", p=P)
    xh_r = xh.ap().rearrange("p (t n) -> p t n", n=CH)

    ln_state = {"g_sums": 0, "a_rows": 0}

    def ln_block(src, width, dst, dst_off, dst_stride, sq_src):
        pl.wait("scalar", "peg", ln_state["g_sums"])
        pl.op("scalar", lambda a, src=src, sq=sq_src: a.activation(
            sq[:, :], src[:, :], AF.Square))
        a_sq = fence_act()
        pl.wait("tensor", "ac", a_sq)
        pl.pe_use(0, 1)

        def mm_sums(t, src=src, sq=sq_src, width=width):
            for kt in range(KT):
                t.matmul(bank[0][0:1, 0:width], ones_col[:, :1],
                         src[:, kt * width:(kt + 1) * width],
                         start=(kt == 0), stop=(kt == KT - 1))
            for kt in range(KT):
                ins = t.matmul(bank[1][0:1, 0:width], ones_col[:, :1],
                               sq[:, kt * width:(kt + 1) * width],
                               start=(kt == 0), stop=(kt == KT - 1))
            if width < 128:
                ins = drain_mm(t)
            return ins
        g_sums = pl.op("tensor", mm_sums, "peg")
        ln_state["g_sums"] = g_sums
        pl.wait("vector", "peg", g_sums)
        pl.wait("vector", "ac", ln_state["a_rows"])

        ev1 = pl.inc("evg")
        pl.rec("I", "vector", "evg", 1)
        h_evg = pl.h["evg"]

        def st1(v, width=width):
            v.tensor_scalar(strow[:1, 0:width], bank[0][0:1, 0:width],
                            1.0 / DM, None, OP.mult)
            ins = v.tensor_scalar(strow[:1, CH:CH + width],
                                  bank[1][0:1, 0:width],
                                  1.0 / DM, None, OP.mult)
            ins.then_inc(h_evg, 1)
            v.tensor_tensor(strow[:1, 2 * CH:2 * CH + width],
                            strow[:1, 0:width], strow[:1, 0:width], OP.mult)
            v.tensor_tensor(strow[:1, CH:CH + width],
                            strow[:1, CH:CH + width],
                            strow[:1, 2 * CH:2 * CH + width],
                            OP.subtract)
            return v.tensor_scalar(strow[:1, CH:CH + width],
                                   strow[:1, CH:CH + width], EPS, None,
                                   OP.add)
        pl.op("vector", st1)
        d_var = fence_dv()
        pl.bank_done(ev1, 0, 1)
        pl.wait("scalar", "dv", d_var)

        def st2(a, width=width):
            a.activation(strow[:1, CH:CH + width], strow[:1, CH:CH + width],
                         AF.Ln)
            a.activation(strow[:1, CH:CH + width], strow[:1, CH:CH + width],
                         AF.Exp, scale=-0.5)
            a.activation(strow16[:1, 0:width], strow[:1, 0:width], AF.Copy)
            return a.activation(strow16[:1, CH:CH + width],
                                strow[:1, CH:CH + width], AF.Copy)
        pl.op("scalar", st2)
        a_rows = fence_act()
        ln_state["a_rows"] = a_rows
        pl.wait("tensor", "ac", a_rows)
        pl.pe_use(2, 3)

        def mm_bc(t, width=width):
            t.matmul(bank[2][:, 0:width], ones_row[:1, :],
                     strow16[:1, 0:width], start=True, stop=True)
            ins = t.matmul(bank[3][:, 0:width], ones_row[:1, :],
                           strow16[:1, CH:CH + width], start=True, stop=True)
            if width < 128:
                ins = drain_mm(t)
            return ins
        g_bc = pl.op("tensor", mm_bc, "peg")
        pl.wait("vector", "peg", g_bc)

        ev2 = pl.inc("evg")
        pl.rec("I", "vector", "evg", 1)

        def apply_ln(v, src=src, dst=dst, off=dst_off, stride=dst_stride,
                     width=width):
            for kt in range(KT):
                ins = v.tensor_tensor(lnsc[:, 0:width],
                                      src[:, kt * width:(kt + 1) * width],
                                      bank[2][:, 0:width], OP.subtract)
                if kt == KT - 1:
                    ins.then_inc(h_evg, 1)
                ins2 = v.tensor_tensor(
                    dst[:, off + kt * stride:off + kt * stride + width],
                    lnsc[:, 0:width], bank[3][:, 0:width], OP.mult)
            return ins2
        pl.op("vector", apply_ln)
        d_xn = fence_dv()
        pl.bank_done(ev2, 2, 3)
        return d_xn

    pl.dma("gpsimd", "lx", xh_r, x_r[:, :, 0:CH].transpose([1, 0, 2]))
    l_hal = pl.dma("gpsimd", "lx", xh_r[:, :, 0:4],
                   xhalo_t.rearrange("p (t n) -> p t n", n=4))
    pl.wait("scalar", "lx", l_hal)
    pl.wait("vector", "lx", l_hal)
    pl.wait("tensor", "lx", l_hal)
    pl.wait("tensor", "dv", d_ones)
    d_halo = ln_block(xh.ap(), CH, xh.ap(), 0, CH, xsq.ap())

    def compact_halo(v):
        return v.tensor_copy(
            xnh.ap().rearrange("p (t n) -> p t n", n=4),
            xh.ap().rearrange("p (t n) -> p t n", n=CH)[:, :, 0:4])
    d_halo = pl.op("vector", compact_halo, "dv")

    xn_applied = []
    for c in range(NCH):
        if c > 0:
            pl.wait("gpsimd", "dv", xn_applied[c - 1])
        else:
            pl.wait("gpsimd", "dv", d_halo)
        l_c = pl.dma("gpsimd", "lx", xh_r,
                     x_r[:, :, c * CH:(c + 1) * CH].transpose([1, 0, 2]))
        pl.wait("scalar", "lx", l_c)
        pl.wait("vector", "lx", l_c)
        pl.wait("tensor", "lx", l_c)
        xn_applied.append(
            ln_block(xh.ap(), CH, xn.ap(), c * CH, TOK, xsq.ap()))

    es_dbc = contextlib.ExitStack()
    dbc_d = es_dbc.enter_context(
        nc.sbuf_tensor("dbc_d", [DTR, TOK], F32, side="right"))
    bc_d = es_dbc.enter_context(
        nc.sbuf_tensor("bc_d", [32, TOK], F32, side="right"))
    es_xc = contextlib.ExitStack()
    xc = es_xc.enter_context(
        nc.sbuf_tensor("xc", [P, KT * 2052], F16, side="right"))

    es_p1 = contextlib.ExitStack()
    wl = es_p1.enter_context(nc.sbuf_tensor("wl", [P, 2 * KT * P], F16))
    rstg = es_p1.enter_context(nc.sbuf_tensor("rstg", [P, 2 * CH], F16))

    pl.wait("tensor", "dv", xn_applied[NCH - 1])
    wl_r = wl.ap().rearrange("p (t n) -> p t n", n=P)
    wl_used = {}
    rst_used = {}
    ev_xc_last = 0
    for m in range(32):
        slot = m % 2
        if m >= 2:
            pl.wait("gpsimd", "peg", wl_used[m - 2])
        l_w = pl.dma("gpsimd", "lw",
                     wl_r[:, slot * KT:(slot + 1) * KT, :], w_in_t[m])
        pl.wait("tensor", "lw", l_w)
        for c in range(NCH):
            bk = (m * NCH + c) % 2
            pl.pe_use(bk)

            def mm_xz(t, c=c, slot=slot, bk=bk):
                for kt in range(KT):
                    ins = t.matmul(
                        bank[bk][:, :],
                        wl[:, (slot * KT + kt) * P:(slot * KT + kt + 1) * P],
                        xn[:, kt * TOK + c * CH:kt * TOK + (c + 1) * CH],
                        start=(kt == 0), stop=(kt == KT - 1))
                return ins
            g = pl.op("tensor", mm_xz, "peg")
            pl.wait("vector", "peg", g)
            if m < KT:
                ev = pl.evac("vector",
                             lambda v, m=m, c=c, bk=bk: v.tensor_copy(
                                 xc[:, m * 2052 + 4 + c * CH:
                                    m * 2052 + 4 + (c + 1) * CH],
                                 bank[bk][:, :]), [bk])
                ev_xc_last = ev
            else:
                rs = (m * NCH + c) % 2
                ridx = (m - KT) * NCH + c
                if ridx >= 2:
                    pl.wait("vector", "sr", rst_used[ridx - 2])
                ev = pl.evac("vector",
                             lambda v, bk=bk, rs=rs: v.tensor_copy(
                                 rstg[:, rs * CH:(rs + 1) * CH],
                                 bank[bk][:, :]), [bk])
                pl.wait("sync", "evg", ev)
                rst_used[ridx] = pl.dma(
                    "sync", "sr",
                    r_d[(m - KT) * P:(m - KT + 1) * P, c * CH:(c + 1) * CH],
                    rstg[:, rs * CH:(rs + 1) * CH])
        if m < KT:
            pl.pe_use(4)

            def mm_halo(t, slot=slot):
                for kt in range(KT):
                    t.matmul(
                        bank[4][:, 0:4],
                        wl[:, (slot * KT + kt) * P:(slot * KT + kt + 1) * P],
                        xnh[:, kt * 4:(kt + 1) * 4],
                        start=(kt == 0), stop=(kt == KT - 1))
                return drain_mm(t)
            g2 = pl.op("tensor", mm_halo, "peg")
            wl_used[m] = g2
            pl.wait("vector", "peg", g2)
            ev = pl.evac("vector", lambda v, m=m: v.tensor_copy(
                xc[:, m * 2052:m * 2052 + 4], bank[4][:, 0:4]), [4])
            ev_xc_last = ev
        else:
            wl_used[m] = g

    es_p1.close()
    es_p0.close()

    if DEBUG:
        xhl_dbg = nc.dram_tensor("xhl_dbg", [P, KT * 4], F16,
                                 kind="ExternalOutput").ap()
        st_dbg = nc.dram_tensor("st_dbg", [1, 3 * CH], F32,
                                kind="ExternalOutput").ap()
        st16_dbg = nc.dram_tensor("st16_dbg", [1, 2 * CH], F16,
                                  kind="ExternalOutput").ap()
        pl.wait("sync", "dv", d_halo)
        pl.dma("sync", "sm", xhl_dbg, xnh[:, :])
        pl.dma("sync", "sm", st_dbg, strow[:1, :])
        pl.dma("sync", "sm", st16_dbg, strow16[:1, :])
        xch_dbg = nc.dram_tensor("xch_dbg", [P, KT * 8], F16,
                                 kind="ExternalOutput").ap()
        pl.wait("sync", "evg", ev_xc_last)
        pl.dma("sync", "sm", xch_dbg,
               xc.ap().rearrange("p (t n) -> p t n", n=2052)[:, :, 0:8])
        xnh_dbg = nc.dram_tensor("xnh_dbg", [P, KT * 4], F16,
                                 kind="ExternalOutput").ap()
        pl.dma("sync", "sm", xnh_dbg, xnh[:, :])
        xn_dbg = nc.dram_tensor("xn_dbg", [DM, TOK], F16,
                                kind="ExternalOutput").ap()
        pl.wait("sync", "dv", xn_applied[NCH - 1])
        pl.dma("sync", "sm",
               xn_dbg.rearrange("(t p) n -> t p n", p=P).transpose([1, 0, 2]),
               xn.ap().rearrange("p (t n) -> p t n", n=TOK))

    es_p2 = contextlib.ExitStack()
    cw = es_p2.enter_context(nc.sbuf_tensor("cw", [P, 2 * 4 * KT * P], F16))
    xstg = es_p2.enter_context(nc.sbuf_tensor("xstg", [P, 2 * CH], F32))

    ev_xcf = fence_dv()
    pl.wait("vector", "evg", 0) if False else None
    pl.wait("tensor", "dv", ev_xcf)
    pl.wait("tensor", "evg", ev_xc_last)
    pl.wait("gpsimd", "peg", wl_used[31])
    pl.wait("vector", "sr", pl.val("sr"))
    cw_r = cw.ap().rearrange("p (s k t n) -> p s k t n", s=2, k=4, n=P)
    cw_used = {}
    xst_used = {}
    for m in range(KT):
        slot = m % 2
        if m >= 2:
            pl.wait("gpsimd", "peg", cw_used[m - 2])
        for kk in range(4):
            l_cw = pl.dma("gpsimd", "lw", cw_r[:, slot, kk], convw_t[kk, m])
        pl.wait("tensor", "lw", l_cw)
        for c in range(NCH):
            bk = (m * NCH + c) % 2
            pl.pe_use(bk)

            def mm_conv(t, c=c, slot=slot, bk=bk):
                for kk in range(4):
                    for kt in range(KT):
                        ins = t.matmul(
                            bank[bk][:, :],
                            cw[:, ((slot * 4 + kk) * KT + kt) * P:
                               ((slot * 4 + kk) * KT + kt + 1) * P],
                            xc[:, kt * 2052 + c * CH + kk + 1:
                               kt * 2052 + c * CH + kk + 1 + CH],
                            start=(kk == 0 and kt == 0),
                            stop=(kk == 3 and kt == KT - 1))
                return ins
            g = pl.op("tensor", mm_conv, "peg")
            if c == NCH - 1:
                cw_used[m] = g
            idx = m * NCH + c
            xsl_ = idx % 2
            pl.wait("vector", "peg", g)
            if idx >= 2:
                pl.wait("vector", "sx", xst_used[idx - 2])
            ev = pl.evac("vector",
                         lambda v, bk=bk, m=m, xsl_=xsl_: v.tensor_scalar(
                             xstg[:, xsl_ * CH:(xsl_ + 1) * CH],
                             bank[bk][:, :], cvec[:, m:m + 1], None,
                             OP.add), [bk])
            pl.wait("sync", "evg", ev)
            xst_used[idx] = pl.dma(
                "sync", "sx",
                xs_d[m * P:(m + 1) * P, c * CH:(c + 1) * CH],
                xstg[:, xsl_ * CH:(xsl_ + 1) * CH])

    es_p2.close()
    es_xc.close()
    es_p2 = contextlib.ExitStack()
    wx16 = es_p2.enter_context(nc.sbuf_tensor("wx16", [P, KT * 160], F16))
    xsf = es_p2.enter_context(nc.sbuf_tensor("xsf", [P, KT * CH], F16))
    u16 = es_p2.enter_context(nc.sbuf_tensor("u16", [P, KT * CH], F16))

    xs_r = xs_d.rearrange("(t p) n -> t p n", p=P)
    pl.wait("gpsimd", "sx", pl.val("sx"))
    pl.wait("gpsimd", "peg", cw_used[KT - 1])
    l_wx = pl.dma("gpsimd", "lw",
                  wx16.ap().rearrange("p (t n) -> p t n", n=160), wx_t)
    a_u_prev = 0
    g_dbc_prev = 0
    for c in range(NCH):
        if c > 0:
            pl.wait("gpsimd", "ac", a_u_prev)
        l_u = pl.dma("gpsimd", "lx",
                     xsf.ap().rearrange("p (t n) -> p t n", n=CH),
                     xs_r[:, :, c * CH:(c + 1) * CH].transpose([1, 0, 2]))
        pl.wait("scalar", "lx", l_u)
        pl.wait("scalar", "peg", g_dbc_prev)
        pl.op("scalar", lambda a: a.activation(
            u16[:, :], xsf[:, :], AF.Silu))
        a_u = fence_act()
        a_u_prev = a_u
        pl.wait("tensor", "ac", a_u)
        pl.wait("tensor", "lw", l_wx)
        pl.pe_use(0, 2)

        def mm_dbc(t):
            for kt in range(KT):
                t.matmul(bank[0][:, :],
                         wx16[:, kt * 160:kt * 160 + 128],
                         u16[:, kt * CH:(kt + 1) * CH],
                         start=(kt == 0), stop=(kt == KT - 1))
            for kt in range(KT):
                ins = t.matmul(bank[2][0:32, :],
                               wx16[:, kt * 160 + 128:kt * 160 + 160],
                               u16[:, kt * CH:(kt + 1) * CH],
                               start=(kt == 0), stop=(kt == KT - 1))
            return ins
        g = pl.op("tensor", mm_dbc, "peg")
        g_dbc_prev = g
        pl.wait("vector", "peg", g)
        ev1 = pl.inc("evg")
        pl.rec("I", "vector", "evg", 1)
        h_evg = pl.h["evg"]

        def ev_dbc(v, c=c):
            ins = v.tensor_copy(dbc_d[:, c * CH:(c + 1) * CH], bank[0][:, :])
            ins.then_inc(h_evg, 1)
            return v.tensor_copy(bc_d[0:32, c * CH:(c + 1) * CH],
                                 bank[2][0:32, :])
        ev2 = pl.op("vector", ev_dbc, "evg")
        pl.bank_done(ev1, 0)
        pl.bank_done(ev2, 2)

    pl.wait("sync", "evg", pl.val("evg"))
    pl.dma("sync", "sm", cc1i[0:DTR, :], dbc_d[:, :])
    pl.dma("sync", "sm", cc1i[DTR:160, :], bc_d[0:32, :])
    pl.wait("sync", "sx", pl.val("sx"))
    pl.dma("sync", "sm", cc2i, xs_d[1024:2048, :])
    pl.wait("gpsimd", "sm", pl.val("sm"))
    c_cc1 = pl.op("gpsimd", lambda g: g.collective_compute(
        "AllReduce", OP.add, replica_groups=RG, ins=[cc1i], outs=[cc1o]),
        "cc")
    c_cc2 = pl.op("gpsimd", lambda g: g.collective_compute(
        "AllReduce", OP.add, replica_groups=RG, ins=[cc2i], outs=[cc2o]),
        "cc")
    for _e in Plan.ENG:
        pl.wait(_e, "cc", c_cc2)

    es_p2.close()

    es_p3 = contextlib.ExitStack()
    sb3 = lambda n, shp, dt: es_p3.enter_context(nc.sbuf_tensor(n, shp, dt))
    glhs = sb3("glhs", [P, 2 * KT * CH], F16)
    ulhs = sb3("ulhs", [P, 2 * KT * CH], F16)
    pkg = sb3("pkg", [P, 8 * CH], I16)
    pku = sb3("pku", [P, 8 * CH], I16)
    lo16 = sb3("lo16", [P, CH], I16)
    hi16 = sb3("hi16", [P, CH], I16)
    scg16 = sb3("scg16", [8, 8 * CH], F16)
    scu16 = sb3("scu16", [8, 8 * CH], F16)
    gel = sb3("gel", [P, 2 * CH], F16)
    istg = sb3("istg", [P, 2 * CH], F16)

    gsc_r = gate_sc.rearrange("(w r) n -> r w n", r=8)
    usc_r = up_sc.rearrange("(w r) n -> r w n", r=8)
    deq_done = {0 - 1: 0}
    lhs_used = {}
    ist_used = {}
    ist_ev = {}
    sc_mm_last = {-1: 0}
    pl.wait("vector", "lw", l_e8)
    pl.wait("tensor", "lw", l_e8)
    pl.wait("gpsimd", "peg", g_dbc_prev)
    pl.wait("gpsimd", "ac", a_u_prev)
    for g in range(NG_A):
        slot = g % 2
        pl.wait("gpsimd", "dv", deq_done[g - 1])
        l_pk = pl.dma("gpsimd", "lp",
                      pkg.ap().rearrange("p (a n) -> p a n", n=CH),
                      gate_pk[g])
        l_pk2 = pl.dma("gpsimd", "lp",
                       pku.ap().rearrange("p (a n) -> p a n", n=CH),
                       up_pk[g])
        pl.wait("gpsimd", "peg", sc_mm_last[g - 1])
        l_sc = pl.dma("gpsimd", "ls",
                      scg16.ap().rearrange("r (w n) -> r w n", n=CH),
                      gsc_r[:, :, g * CH:(g + 1) * CH])
        l_sc2 = pl.dma("gpsimd", "ls",
                       scu16.ap().rearrange("r (w n) -> r w n", n=CH),
                       usc_r[:, :, g * CH:(g + 1) * CH])
        pl.wait("tensor", "ls", l_sc2)
        pl.wait("vector", "lp", l_pk2)
        if g >= 2:
            pl.wait("vector", "peg", lhs_used[g - 2])
        for mat, pk, sct, lhs in ((0, pkg, scg16, glhs),
                                  (1, pku, scu16, ulhs)):
            for kw in range(8):
                bsc = 6 + (kw + mat * 8) % 2
                pl.pe_use(bsc)

                def mm_sc(t, sct=sct, kw=kw, bsc=bsc):
                    return t.matmul(bank[bsc][:, :], e8_sb[:, :],
                                    sct[:, kw * CH:(kw + 1) * CH],
                                    start=True, stop=True)
                gq = pl.op("tensor", mm_sc, "peg")
                if mat == 1 and kw == 7:
                    sc_mm_last[g] = gq
                pl.wait("vector", "peg", gq)

                def deq(v, pk=pk, lhs=lhs, kw=kw, slot=slot, bsc=bsc):
                    src = pk[:, kw * CH:(kw + 1) * CH]
                    v.tensor_scalar(lo16[:, :], src, 7, 9,
                                    OP.bitwise_and, OP.logical_shift_left)
                    v.tensor_scalar(hi16[:, :], src, 8, 12,
                                    OP.bitwise_and, OP.logical_shift_left)
                    v.tensor_tensor(lo16[:, :], lo16[:, :], hi16[:, :],
                                    OP.bitwise_or)
                    v.tensor_scalar(hi16[:, :], src, 112, 5,
                                    OP.bitwise_and, OP.logical_shift_left)
                    v.tensor_tensor(
                        lhs[:, (slot * KT + kw) * CH:
                            (slot * KT + kw + 1) * CH],
                        lo16[:, :].bitcast(F16), bank[bsc][:, :], OP.mult)
                    v.tensor_scalar(lo16[:, :], src, 128, 8,
                                    OP.bitwise_and, OP.logical_shift_left)
                    v.tensor_tensor(lo16[:, :], hi16[:, :], lo16[:, :],
                                    OP.bitwise_or)
                    return v.tensor_tensor(
                        lhs[:, (slot * KT + 8 + kw) * CH:
                            (slot * KT + 8 + kw + 1) * CH],
                        lo16[:, :].bitcast(F16), bank[bsc][:, :], OP.mult)
                ev = pl.evac("vector", deq, [bsc])
        h_dv = pl.h["dv"]
        deq_done[g] = pl.inc("dv")
        pl.rec("I", "vector", "dv", 1)
        pl.emit("vector", lambda v, h=h_dv: v.tensor_copy(
            lo16[:, 0:4], hi16[:, 0:4]).then_inc(h, 1))
        if DEBUG and g in (0, 30, 31):
            gt = nc.dram_tensor(f"glhs_dbg{g}", [P, KT * CH], F16,
                                kind="ExternalOutput").ap()
            ut = nc.dram_tensor(f"ulhs_dbg{g}", [P, KT * CH], F16,
                                kind="ExternalOutput").ap()
            pl.wait("sync", "dv", deq_done[g])
            pl.dma("sync", "sm", gt,
                   glhs[:, slot * KT * CH:(slot + 1) * KT * CH])
            s_dbgw = pl.dma("sync", "sm", ut,
                            ulhs[:, slot * KT * CH:(slot + 1) * KT * CH])
            pl.wait("vector", "sm", s_dbgw)

        pl.wait("tensor", "dv", deq_done[g])
        for mi in range(4):
            for c in range(NCH):
                bg = ((mi * NCH + c) % 2) * 2
                pl.pe_use(bg, bg + 1)

                def mm_gu(t, mi=mi, c=c, slot=slot, bg=bg):
                    for kt in range(KT):
                        t.matmul(bank[bg][:, :],
                                 glhs[:, (slot * KT + kt) * CH + mi * P:
                                      (slot * KT + kt) * CH + (mi + 1) * P],
                                 xn[:, kt * TOK + c * CH:
                                    kt * TOK + (c + 1) * CH],
                                 start=(kt == 0), stop=(kt == KT - 1))
                    for kt in range(KT):
                        ins = t.matmul(
                            bank[bg + 1][:, :],
                            ulhs[:, (slot * KT + kt) * CH + mi * P:
                                 (slot * KT + kt) * CH + (mi + 1) * P],
                            xn[:, kt * TOK + c * CH:kt * TOK + (c + 1) * CH],
                            start=(kt == 0), stop=(kt == KT - 1))
                    return ins
                gg = pl.op("tensor", mm_gu, "peg")
                if mi == 3 and c == NCH - 1:
                    lhs_used[g] = gg
                isl = (mi * NCH + c) % 2
                idx = (g * 4 + mi) * NCH + c
                pl.wait("scalar", "peg", gg)
                if idx >= 2:
                    pl.wait("scalar", "evg", ist_ev[idx - 2])
                pl.op("scalar", lambda a, bg=bg, isl=isl:
                      a.activation(gel[:, isl * CH:(isl + 1) * CH],
                                   bank[bg][:, :], AF.Gelu_apprx_tanh))
                a_gel = fence_act()
                pl.wait("vector", "ac", a_gel)
                if idx >= 2:
                    pl.wait("vector", "si", ist_used[idx - 2])

                def mul_iu(v, bg=bg, isl=isl):
                    return v.tensor_tensor(istg[:, isl * CH:(isl + 1) * CH],
                                           gel[:, isl * CH:(isl + 1) * CH],
                                           bank[bg + 1][:, :], OP.mult)
                ev2 = pl.op("vector", mul_iu, "evg")
                pl.bank_done(ev2, bg, bg + 1)
                ist_ev[idx] = ev2
                pl.wait("sync", "evg", ev2)
                it = g * 4 + mi
                ist_used[idx] = pl.dma(
                    "sync", "si",
                    int_d[it * P:(it + 1) * P, c * CH:(c + 1) * CH],
                    istg[:, isl * CH:(isl + 1) * CH])

    es_p3.close()
    es_xn.close()

    es_scan = contextlib.ExitStack()
    dbc16 = es_scan.enter_context(nc.sbuf_tensor("dbc16", [DTR, L], F16))
    bc16 = es_scan.enter_context(nc.sbuf_tensor("bc16", [32, L], F16))
    oh_sb = es_scan.enter_context(nc.sbuf_tensor("oh_sb", [32, 32 * P], F16))
    pl.wait("gpsimd", "peg", lhs_used[NG_A - 1])
    pl.wait("gpsimd", "si", pl.val("si"))
    l_oh = pl.dma("gpsimd", "lw", oh_sb[:, :], oh_t)

    es_x = contextlib.ExitStack()
    ta = es_x.enter_context(nc.sbuf_tensor("ta", [P, TOK], F32))
    tb = es_x.enter_context(nc.sbuf_tensor("tb", [P, TOK], F32))
    tcx = es_x.enter_context(nc.sbuf_tensor("tcx", [P, TOK], F32))

    pl.wait("sync", "cc", c_cc1)
    pl.wait("sync", "peg", lhs_used[NG_A - 1])
    pl.wait("sync", "si", pl.val("si"))
    pl.dma("sync", "le", ta[:, :], cc1o[0:DTR, :])
    l2 = pl.dma("sync", "le", tb[0:32, :], cc1o[DTR:160, :])
    pl.wait("vector", "le", l2)

    def sub1(v):
        v.tensor_sub(ta[:, :], ta[:, :], dbc_d[:, :])
        return v.tensor_sub(tb[0:32, :], tb[0:32, :], bc_d[0:32, :])
    d_s1 = pl.op("vector", sub1, "dv")
    pl.wait("sync", "dv", d_s1)
    s_db = pl.dma_pair("sync", "sm", [
        (dbg_d[:, 0:TOK], dbc_d[:, :], dbg_d[:, TOK:L], dbc_d[:, :]),
        (dbg_d[:, TOK:L], ta[:, :], dbg_d[:, 0:TOK], ta[:, :]),
        (bcg_d[0:32, 0:TOK], bc_d[0:32, :], bcg_d[0:32, TOK:L],
         bc_d[0:32, :]),
        (bcg_d[0:32, TOK:L], tb[0:32, :], bcg_d[0:32, 0:TOK], tb[0:32, :]),
    ])
    pl.wait("gpsimd", "sm", s_db)
    pl.dma("gpsimd", "le", dbc16[:, :], dbg_d)
    l_db2 = pl.dma("gpsimd", "le", bc16[0:32, :], bcg_d)
    h_dvdb = pl.h["dv"]
    d_db = pl.inc("dv")
    pl.wait("vector", "le", l_db2)
    pl.rec("I", "vector", "dv", 1)
    pl.emit("vector", lambda v, h=h_dvdb: v.tensor_copy(
        tcx[:, 0:256], ta[:, 0:256]).then_inc(h, 1))

    pl.wait("sync", "cc", c_cc2)
    d_sub_prev = 0
    sm_prev = 0
    for j in range(DTMY):
        if j > 0:
            pl.wait("sync", "dv", d_sub_prev)
        l3 = pl.dma("sync", "le", ta[:, :], cc2o[j * P:(j + 1) * P, :])
        l4 = pl.dma("sync", "le", tb[:, :],
                    xs_d[1024 + j * P:1024 + (j + 1) * P, :])
        pl.wait("vector", "le", l4)
        if j > 0:
            pl.wait("vector", "sm", sm_prev)
        d_s = pl.op("vector", lambda v: v.tensor_sub(
            tcx[:, :], ta[:, :], tb[:, :]), "dv")
        pl.wait("sync", "dv", d_s)
        sm_prev = pl.dma("sync", "sm", pxs_d[j * P:(j + 1) * P, :],
                         tcx[:, :])
        d_sub_prev = d_s
    pl.wait("sync", "sm", sm_prev)
    pl.dma_pair("sync", "sm", [
        (xsc_d[:, TOK:L], pxs_d, xsc_d[:, 0:TOK], pxs_d),
        (xsc_d[:, 0:TOK], xs_d[0:1024, :], xsc_d[:, TOK:L],
         xs_d[0:1024, :]),
    ])
    es_x.close()
    es_dbc.close()

    if DEBUG:
        dbc_dbg = nc.dram_tensor("dbc_dbg", [DTR, L], F16,
                                 kind="ExternalOutput").ap()
        bc_dbg = nc.dram_tensor("bc_dbg", [32, L], F16,
                                kind="ExternalOutput").ap()
        xs_dbg = nc.dram_tensor("xs_dbg", [DM, TOK], F32,
                                kind="ExternalOutput").ap()
        y_dbg = nc.dram_tensor("y_dbg", [1024, L], F32,
                               kind="ExternalOutput").ap()
        pl.wait("sync", "le", pl.val("le"))
        pl.dma("sync", "sm", dbc_dbg, dbc16[:, :])
        pl.dma("sync", "sm", bc_dbg, bc16[0:32, :])
        pl.dma("sync", "sm", xs_dbg, xs_d)

    es_p4 = contextlib.ExitStack()
    sb4 = lambda n, shp, dt: es_p4.enter_context(nc.sbuf_tensor(n, shp, dt))
    xsb = sb4("xsb", [P, 2 * DTMY * CH], F32)
    u8t = sb4("u8t", [P, DTMY * CH], F16)
    d16 = sb4("d16", [P, DTMY * CH], F16)
    du16 = sb4("du16", [P, DTMY * CH], F16)
    ytile = sb4("ytile", [P, DTMY * CH], F32)
    spr = sb4("spr", [P, 2 * CH], F32)
    spa = sb4("spa", [P, 2 * CH], F32)
    dAs = sb4("dAs", [P, 2 * CH], F32)
    dbu = sb4("dbu", [P, 2 * CH], F32)
    hbuf = sb4("hbuf", [P, 2 * CH], F32)
    hcs = sb4("hcs", [P, 2 * CH], F32)
    bsb = sb4("bsb", [P, 2 * CH], F16)
    csb = sb4("csb", [P, 2 * CH], F16)

    pl.wait("sync", "sm", pl.val("sm"))
    pl.wait("scalar", "dv", d_aneg)
    pl.wait("tensor", "dv", d_db)
    pl.wait("tensor", "lw", l_wdt)
    pl.wait("tensor", "lw", l_oh)
    pl.wait("gpsimd", "dv", d_aneg)
    xsl_done = {}
    y_stored = {}
    H = {"a_dA": {}, "p_dbu": {}, "d_h": {}, "p_hc": {}, "p_hc_flat": {},
         "d_y": {}, "dd": {}}
    d_y_last = 0
    po_chunk_end = {}
    u = -1
    for c in range(SC):
        slot = c % 2
        if c >= 2:
            pl.wait("sync", "ac", xsl_done[c - 2])
        for j in range(DTMY):
            l_xs = pl.dma(
                "sync", "le",
                xsb[:, (slot * DTMY + j) * CH:(slot * DTMY + j + 1) * CH],
                xsc_d[j * P:(j + 1) * P, c * CH:(c + 1) * CH])
        pl.wait("scalar", "le", l_xs)
        if c > 0:
            pl.wait("scalar", "dv", H["dd"][(c - 1, DTMY - 1)])
        a_u = pl.op("scalar", lambda a, slot=slot: a.activation(
            u8t[:, :], xsb[:, slot * DTMY * CH:(slot + 1) * DTMY * CH],
            AF.Silu), "ac")
        xsl_done[c] = a_u
        for j in range(DTMY):
            bk = j % 2
            pl.pe_use(bk)

            def mm_dl(t, j=j, c=c, bk=bk):
                return t.matmul(bank[bk][:, :], wdt16[:, j * P:(j + 1) * P],
                                dbc16[:, c * CH:(c + 1) * CH],
                                start=True, stop=True)
            gd = pl.op("tensor", mm_dl, "peg")
            pl.wait("scalar", "peg", gd)
            ssl = j % 2
            k_sp = c * DTMY + j
            if k_sp >= 2:
                pl.wait("scalar", "dv", H["dd"][(c, j - 2) if j >= 2 else
                                                (c - 1, DTMY + j - 2)])
            ev1 = pl.inc("evg")
            pl.rec("I", "scalar", "evg", 1)
            h_evg3 = pl.h["evg"]

            def sp_act(a, j=j, bk=bk, ssl=ssl):
                a.activation(spr[:, ssl * CH:(ssl + 1) * CH], bank[bk][:, :],
                             AF.Relu, bias=bdt[:, j:j + 1])
                ins = a.activation(spa[:, ssl * CH:(ssl + 1) * CH],
                                   bank[bk][:, :], AF.Abs,
                                   bias=bdt[:, j:j + 1])
                ins.then_inc(h_evg3, 1)
                a.activation(spa[:, ssl * CH:(ssl + 1) * CH],
                             spa[:, ssl * CH:(ssl + 1) * CH], AF.Exp,
                             scale=-1.0)
                return a.activation(spa[:, ssl * CH:(ssl + 1) * CH],
                                    spa[:, ssl * CH:(ssl + 1) * CH], AF.Ln,
                                    bias=1.0)
            pl.op("scalar", sp_act)
            a_sp = fence_act()
            pl.bank_done(ev1, bk)
            pl.wait("vector", "ac", a_sp)
            if c >= 1 and j == 0:
                pl.wait("vector", "po", po_chunk_end[c - 1])

            def mk_d(v, j=j, ssl=ssl):
                v.tensor_tensor(d16[:, j * CH:(j + 1) * CH],
                                spr[:, ssl * CH:(ssl + 1) * CH],
                                spa[:, ssl * CH:(ssl + 1) * CH], OP.add)
                return v.tensor_tensor(
                    du16[:, j * CH:(j + 1) * CH], d16[:, j * CH:(j + 1) * CH],
                    u8t[:, j * CH:(j + 1) * CH], OP.mult)
            H["dd"][(c, j)] = pl.op("vector", mk_d, "dv")
        d_prep = fence_dv()

        if c == 0:
            for s_ in (0, 1):
                pl.wait("scalar", "dv", d_prep)

                def wm_dA(a, s_=s_):
                    return a.activation(dAs[:, s_ * CH:(s_ + 1) * CH],
                                        d16[:, 0:CH], AF.Exp,
                                        scale=aneg[:, 0:1])
                pl.op("scalar", wm_dA)
                wm_a = fence_act()
                pl.wait("gpsimd", "ac", wm_a)

                def wm_dbu(g, s_=s_):
                    return g.tensor_tensor(dbu[:, s_ * CH:(s_ + 1) * CH],
                                           du16[:, 0:CH],
                                           dAs[:, s_ * CH:(s_ + 1) * CH],
                                           OP.mult)
                wm_p = pl.op("gpsimd", wm_dbu, "po")
                pl.wait("vector", "po", wm_p)

                def wm_h(v, s_=s_):
                    v.tensor_tensor_scan(hbuf[:, s_ * CH:(s_ + 1) * CH],
                                         dAs[:, s_ * CH:(s_ + 1) * CH],
                                         dbu[:, s_ * CH:(s_ + 1) * CH],
                                         0.0, OP.mult, OP.add)
                    v.tensor_copy(strow[0:1, 0:64], strow[0:1, 64:128])
                    v.tensor_copy(strow[0:1, 0:64], strow[0:1, 64:128])
                    v.tensor_copy(strow[0:1, 0:64], strow[0:1, 64:128])
                    return v.tensor_copy(hcs[:, s_ * CH:s_ * CH + 64],
                                         hbuf[:, s_ * CH:s_ * CH + 64])
                wm_d = pl.op("vector", wm_h, "dv")
                pl.wait("gpsimd", "dv", wm_d)

                def wm_hc(g, s_=s_):
                    return g.tensor_tensor(hcs[:, s_ * CH:(s_ + 1) * CH],
                                           hbuf[:, s_ * CH:(s_ + 1) * CH],
                                           dbu[:, s_ * CH:(s_ + 1) * CH],
                                           OP.mult)
                pl.op("gpsimd", wm_hc, "po")

        for n in range(NST):
            bb, bcx = 2 + (n % 2) * 2, 3 + (n % 2) * 2
            pl.pe_use(bb, bcx)

            def mm_bc2(t, n=n, c=c, bb=bb, bcx=bcx):
                t.matmul(bank[bb][:, :], oh_sb[:, n * P:(n + 1) * P],
                         bc16[0:32, c * CH:(c + 1) * CH],
                         start=True, stop=True)
                return t.matmul(bank[bcx][:, :],
                                oh_sb[:, (16 + n) * P:(17 + n) * P],
                                bc16[0:32, c * CH:(c + 1) * CH],
                                start=True, stop=True)
            gb = pl.op("tensor", mm_bc2, "peg")
            pl.wait("scalar", "peg", gb)
            if n == 0:
                pl.wait("scalar", "dv", d_prep)
            sbl = n % 2
            un = c * NST + n
            if un >= 2:
                pl.wait("scalar", "po", H["p_hc"][(un - 2, DTMY - 1)])
            def bc_ev(a, bb=bb, bcx=bcx, sbl=sbl):
                a.activation(bsb[:, sbl * CH:(sbl + 1) * CH],
                             bank[bb][:, :], AF.Copy)
                return a.activation(csb[:, sbl * CH:(sbl + 1) * CH],
                                    bank[bcx][:, :], AF.Copy)
            pl.op("scalar", bc_ev)
            ev_bc = fence_act("evg", 2)
            pl.bank_done(ev_bc, bb, bcx)

            for j in range(DTMY):
                u += 1
                ss = u % 2
                col = j * NST + n
                if u >= 2:
                    pl.wait("scalar", "dv", H["d_h"][u - 2])

                def mk_dA(a, j=j, ss=ss, col=col):
                    return a.activation(dAs[:, ss * CH:(ss + 1) * CH],
                                        d16[:, j * CH:(j + 1) * CH], AF.Exp,
                                        scale=aneg[:, col:col + 1])
                pl.op("scalar", mk_dA)
                H["a_dA"][u] = fence_act()
                pl.wait("gpsimd", "ac", H["a_dA"][u])
                pl.wait("gpsimd", "evg", ev_bc)
                if u >= 2:
                    pl.wait("gpsimd", "dv", H["d_h"][u - 2])

                def mk_dbu(g, j=j, ss=ss, sbl=sbl):
                    return g.tensor_tensor(dbu[:, ss * CH:(ss + 1) * CH],
                                           du16[:, j * CH:(j + 1) * CH],
                                           bsb[:, sbl * CH:(sbl + 1) * CH],
                                           OP.mult)
                H["p_dbu"][u] = pl.op("gpsimd", mk_dbu, "po")
                pl.wait("vector", "ac", H["a_dA"][u])
                pl.wait("vector", "po", H["p_dbu"][u])
                if u >= 2:
                    pl.wait("vector", "po", H["p_hc_flat"][u - 2])
                ccol = n * DTMY + j

                def mk_h(v, ss=ss, ccol=ccol):
                    v.tensor_tensor_scan(hbuf[:, ss * CH:(ss + 1) * CH],
                                         dAs[:, ss * CH:(ss + 1) * CH],
                                         dbu[:, ss * CH:(ss + 1) * CH],
                                         carry[:, ccol:ccol + 1],
                                         OP.mult, OP.add)
                    v.tensor_copy(strow[0:1, 0:64], strow[0:1, 64:128])
                    v.tensor_copy(strow[0:1, 0:64], strow[0:1, 64:128])
                    v.tensor_copy(strow[0:1, 0:64], strow[0:1, 64:128])
                    return v.tensor_copy(
                        carry[:, ccol:ccol + 1],
                        hbuf[:, ss * CH + CH - 1:ss * CH + CH])
                H["d_h"][u] = pl.op("vector", mk_h, "dv")
                pl.wait("gpsimd", "dv", H["d_h"][u])
                if u >= 2:
                    pl.wait("gpsimd", "dv", H["d_y"][u - 2])

                def mk_hc(g, ss=ss, sbl=sbl):
                    return g.tensor_tensor(hcs[:, ss * CH:(ss + 1) * CH],
                                           hbuf[:, ss * CH:(ss + 1) * CH],
                                           csb[:, sbl * CH:(sbl + 1) * CH],
                                           OP.mult)
                ph = pl.op("gpsimd", mk_hc, "po")
                H["p_hc"][(un, j)] = ph
                H["p_hc_flat"][u] = ph
                pl.wait("vector", "po", ph)
                if n == 0 and c > 0:
                    pl.wait("vector", "sy", y_stored[c - 1])

                def mk_y(v, n=n, j=j, ss=ss):
                    if n == 0:
                        return v.tensor_copy(ytile[:, j * CH:(j + 1) * CH],
                                             hcs[:, ss * CH:(ss + 1) * CH])
                    return v.tensor_tensor(ytile[:, j * CH:(j + 1) * CH],
                                           ytile[:, j * CH:(j + 1) * CH],
                                           hcs[:, ss * CH:(ss + 1) * CH],
                                           OP.add)
                H["d_y"][u] = pl.op("vector", mk_y, "dv")
        po_chunk_end[c] = pl.val("po")
        d_yf = fence_dv()
        pl.wait("sync", "dv", d_yf)
        for j in range(DTMY):
            y_stored[c] = pl.dma(
                "sync", "sy",
                y_d[j * P:(j + 1) * P, c * CH:(c + 1) * CH],
                ytile[:, j * CH:(j + 1) * CH])

    if DEBUG:
        an_dbg = nc.dram_tensor("an_dbg", [P, DTMY * NST], F32,
                                kind="ExternalOutput").ap()
        wdt_dbg = nc.dram_tensor("wdt_dbg", [DTR, 1024], F16,
                                 kind="ExternalOutput").ap()
        d16_dbg = nc.dram_tensor("d16_dbg", [P, DTMY * CH], F16,
                                 kind="ExternalOutput").ap()
        u8_dbg = nc.dram_tensor("u8_dbg", [P, DTMY * CH], F16,
                                kind="ExternalOutput").ap()
        ca_dbg = nc.dram_tensor("ca_dbg", [P, NST * DTMY], F32,
                                kind="ExternalOutput").ap()
        pl.wait("sync", "sy", pl.val("sy"))
        pl.dma("sync", "sm", ca_dbg, carry[:, :])
        pl.dma("sync", "sm", an_dbg, aneg[:, :])
        pl.dma("sync", "sm", wdt_dbg, wdt16[:, :])
        pl.dma("sync", "sm", d16_dbg, d16[:, :])
        pl.dma("sync", "sm", u8_dbg, u8t[:, :])
        pl.dma("sync", "sm", y_dbg, y_d)
    es_p4.close()
    es_scan.close()

    es_p5 = contextlib.ExitStack()
    sb5 = lambda n, shp, dt: es_p5.enter_context(nc.sbuf_tensor(n, shp, dt))
    gacc = sb5("gacc", [P, KT * TOK], F32)
    es_p5s = contextlib.ExitStack()
    sb5s = lambda n, shp, dt: es_p5s.enter_context(nc.sbuf_tensor(n, shp, dt))
    dlo = sb5s("dlo", [P, 4 * TOK], F16)
    dhi = sb5s("dhi", [P, 4 * TOK], F16)
    dpk = sb5s("dpk", [P, TOK], I16)
    dl16 = sb5s("dl16", [P, TOK], I16)
    dh16 = sb5s("dh16", [P, TOK], I16)
    dsc16 = sb5s("dsc16", [8, TOK], F16)
    intb = sb5s("intb", [P, 2 * 8 * CH], F16)

    dsc_r = down_sc.rearrange("(t r) n -> t r n", r=8)
    int_r = int_d.rearrange("(t p) n -> t p n", p=P)
    pl.wait("sync", "sy", pl.val("sy"))
    dq_done = {-1: 0}
    intb_used = {}
    mmdn_last = {-1: 0}
    ev_gacc = 0
    k_dq = 0
    for gb in range(NG_B):
        for j in range(4):
            td = 4 * gb + j
            pl.wait("gpsimd", "dv", dq_done[k_dq - 1])
            l_dp = pl.dma("gpsimd", "lp", dpk[:, :], down_pk[td])
            l_ds = pl.dma("gpsimd", "ls", dsc16[:, :], dsc_r[td])
            pl.wait("tensor", "ls", l_ds)
            pl.wait("vector", "lp", l_dp)
            if j == 0:
                pl.wait("vector", "peg", mmdn_last[gb - 1])

            def deq_a(v):
                v.tensor_scalar(dl16[:, :], dpk[:, :], 7, 9,
                                OP.bitwise_and, OP.logical_shift_left)
                v.tensor_scalar(dh16[:, :], dpk[:, :], 8, 12,
                                OP.bitwise_and, OP.logical_shift_left)
                return v.tensor_tensor(dl16[:, :], dl16[:, :], dh16[:, :],
                                       OP.bitwise_or)
            d_a = pl.op("vector", deq_a, "dv")
            pl.wait("tensor", "dv", d_a)
            for s in range(4):
                bsc = 6 + s % 2
                pl.pe_use(bsc)

                def mm_ds(t, s=s, bsc=bsc):
                    return t.matmul(bank[bsc][:, :], e8_sb[:, :],
                                    dsc16[:, s * CH:(s + 1) * CH],
                                    start=True, stop=True)
                gq = pl.op("tensor", mm_ds, "peg")
                pl.wait("vector", "peg", gq)
                pl.evac("vector", lambda v, j=j, s=s, bsc=bsc:
                        v.tensor_tensor(
                            dlo[:, j * TOK + s * CH:j * TOK + (s + 1) * CH],
                            dl16[:, s * CH:(s + 1) * CH].bitcast(F16),
                            bank[bsc][:, :], OP.mult), [bsc])

            def deq_b(v):
                v.tensor_scalar(dl16[:, :], dpk[:, :], 112, 5,
                                OP.bitwise_and, OP.logical_shift_left)
                v.tensor_scalar(dh16[:, :], dpk[:, :], 128, 8,
                                OP.bitwise_and, OP.logical_shift_left)
                return v.tensor_tensor(dh16[:, :], dl16[:, :], dh16[:, :],
                                       OP.bitwise_or)
            d_b = pl.op("vector", deq_b, "dv")
            pl.wait("tensor", "dv", d_b)
            for s in range(4):
                bsc = 6 + s % 2
                pl.pe_use(bsc)

                def mm_ds2(t, s=s, bsc=bsc):
                    return t.matmul(bank[bsc][:, :], e8_sb[:, :],
                                    dsc16[:, s * CH:(s + 1) * CH],
                                    start=True, stop=True)
                gq = pl.op("tensor", mm_ds2, "peg")
                pl.wait("vector", "peg", gq)
                pl.evac("vector", lambda v, j=j, s=s, bsc=bsc:
                        v.tensor_tensor(
                            dhi[:, j * TOK + s * CH:j * TOK + (s + 1) * CH],
                            dh16[:, s * CH:(s + 1) * CH].bitcast(F16),
                            bank[bsc][:, :], OP.mult), [bsc])
            h_dvx = pl.h["dv"]
            dq_done[k_dq] = pl.inc("dv")
            pl.rec("I", "vector", "dv", 1)
            pl.emit("vector", lambda v, h=h_dvx: v.tensor_copy(
                dl16[:, 0:CH], dh16[:, 0:CH]).then_inc(h, 1))
            k_dq += 1

        pl.wait("tensor", "dv", dq_done[k_dq - 1])
        for c in range(NCH):
            islot = c % 2
            ic = gb * NCH + c
            if ic >= 2:
                pl.wait("sync", "peg", intb_used[ic - 2])
            for kj in range(8):
                td = 4 * gb + (kj % 4)
                row = td if kj < 4 else 64 + td
                l_int = pl.dma(
                    "sync", "li",
                    intb[:, (islot * 8 + kj) * CH:(islot * 8 + kj + 1) * CH],
                    int_r[row, :, c * CH:(c + 1) * CH])
            pl.wait("tensor", "li", l_int)
            for m in range(KT):
                bk = (m + c * KT) % 4
                pl.pe_use(bk)

                def mm_dn(t, m=m, islot=islot, bk=bk):
                    for kj in range(8):
                        lhs = dlo if kj < 4 else dhi
                        jj = kj % 4
                        ins = t.matmul(
                            bank[bk][:, :],
                            lhs[:, jj * TOK + m * P:jj * TOK + (m + 1) * P],
                            intb[:, (islot * 8 + kj) * CH:
                                 (islot * 8 + kj + 1) * CH],
                            start=(kj == 0), stop=(kj == 7))
                    return ins
                gd = pl.op("tensor", mm_dn, "peg")
                if m == KT - 1:
                    intb_used[ic] = gd
                    if c == NCH - 1:
                        mmdn_last[gb] = gd
                pl.wait("vector", "peg", gd)
                ev_gacc = pl.evac(
                    "vector", lambda v, m=m, c=c, bk=bk, gb=gb:
                    (v.tensor_copy(
                        gacc[:, m * TOK + c * CH:m * TOK + (c + 1) * CH],
                        bank[bk][:, :]) if gb == 0 else
                     v.tensor_tensor(
                        gacc[:, m * TOK + c * CH:m * TOK + (c + 1) * CH],
                        gacc[:, m * TOK + c * CH:m * TOK + (c + 1) * CH],
                        bank[bk][:, :], OP.add)), [bk])

    ev_gaccf = fence_dv()
    es_p5s.close()
    gstg = sb5("gstg", [P, 2 * TOK], F16)
    gs_used = {}
    pl.wait("scalar", "dv", ev_gaccf)
    for m in range(KT):
        gsl = m % 2
        if m >= 2:
            pl.wait("scalar", "sm", gs_used[m - 2])
        pl.op("scalar", lambda a, m=m, gsl=gsl: a.activation(
            gstg[:, gsl * TOK:(gsl + 1) * TOK],
            gacc[:, m * TOK:(m + 1) * TOK], AF.Silu))
        a_g = fence_act()
        pl.wait("sync", "ac", a_g)
        gs_used[m] = pl.dma("sync", "sm", g_d[m * P:(m + 1) * P, :],
                            gstg[:, gsl * TOK:(gsl + 1) * TOK])
    pl.wait("gpsimd", "sy", pl.val("sy"))
    pl.wait("gpsimd", "sm", pl.val("sm"))
    pl.wait("gpsimd", "peg", pl.val("peg"))
    pl.wait("gpsimd", "dv", pl.val("dv"))
    pl.wait("gpsimd", "ac", pl.val("ac"))
    c_cc3 = pl.op("gpsimd", lambda g: g.collective_compute(
        "AllReduce", OP.add, replica_groups=RG, ins=[y_d], outs=[cc3o]),
        "cc")
    for _e in Plan.ENG:
        pl.wait(_e, "cc", c_cc3)
    es_p5.close()

    es_p6 = contextlib.ExitStack()
    sb6 = lambda n, shp, dt: es_p6.enter_context(nc.sbuf_tensor(n, shp, dt))
    yall = sb6("yall", [P, KT * CH], F32)
    xsl = sb6("xsl", [P, KT * CH], F32)
    rl = sb6("rl", [P, KT * CH], F16)
    gl = sb6("gl", [P, KT * CH], F16)
    xl = sb6("xl", [P, KT * CH], F32)
    pys = sb6("pys", [P, CH], F32)
    uep = sb6("uep", [P, 2 * CH], F32)
    srb = sb6("srb", [P, 2 * CH], F16)
    op16 = sb6("op16", [P, KT * CH], F16)
    ostg = sb6("ostg", [P, 2 * CH], F32)
    epsc = sb6("epsc", [P, 2 * CH], F32)
    wol = sb6("wol", [P, 2 * KT * P], F16)

    pl.wait("sync", "cc", c_cc3)
    pl.wait("sync", "sm", pl.val("sm"))
    pl.wait("gpsimd", "sm", pl.val("sm"))
    pl.wait("gpsimd", "peg", mmdn_last[NG_B - 1])
    pl.dma_pair("sync", "le", [
        (ymg_d[0:1024, :], y_d[:, 0:TOK], ymg_d[0:1024, :], y_d[:, TOK:L]),
        (ymg_d[1024:DM, :], cc3o[:, 0:TOK], ymg_d[1024:DM, :],
         cc3o[:, TOK:L]),
    ])
    pl.wait("sync", "le", pl.val("le"))
    r_r = r_d.rearrange("(t p) n -> t p n", p=P)
    g_r = g_d.rearrange("(t p) n -> t p n", p=P)
    wol_r = wol.ap().rearrange("p (t n) -> p t n", n=P)
    ost_used = {}
    wol_used = {}
    d_f_last = 0
    a_ep_last = 0
    d_op_hist = {}
    for c in range(NCH):
        if c > 0:
            pl.wait("sync", "dv", d_f_last)
            pl.wait("sync", "ac", a_ep_last)
        for j in range(DTMY):
            pl.dma("sync", "le", yall[:, j * CH:(j + 1) * CH],
                   ymg_d[j * P:(j + 1) * P, c * CH:(c + 1) * CH])
        d_py = 0
        for j in range(DTMY):
            if d_py:
                pl.wait("sync", "dv", d_py)
            lp2 = pl.dma("sync", "le", pys[:, :],
                         ymg_d[1024 + j * P:1024 + (j + 1) * P,
                               c * CH:(c + 1) * CH])
            pl.wait("vector", "le", lp2)
            d_py = pl.op("vector", lambda v, j=j: v.tensor_sub(
                yall[:, (8 + j) * CH:(9 + j) * CH], pys[:, :],
                yall[:, j * CH:(j + 1) * CH]), "dv")
        l_xs = pl.dma("sync", "le",
                      xsl.ap().rearrange("p (t n) -> p t n", n=CH),
                      xs_r[:, :, c * CH:(c + 1) * CH].transpose([1, 0, 2]))
        l_r = pl.dma("sync", "le",
                     rl.ap().rearrange("p (t n) -> p t n", n=CH),
                     r_r[:, :, c * CH:(c + 1) * CH].transpose([1, 0, 2]))
        l_g = pl.dma("sync", "le",
                     gl.ap().rearrange("p (t n) -> p t n", n=CH),
                     g_r[:, :, c * CH:(c + 1) * CH].transpose([1, 0, 2]))
        l_x = pl.dma("sync", "le",
                     xl.ap().rearrange("p (t n) -> p t n", n=CH),
                     x_r[:, :, c * CH:(c + 1) * CH].transpose([1, 0, 2]))
        d_py = fence_dv()
        pl.wait("scalar", "le", l_x)
        pl.wait("vector", "le", l_x)
        pl.wait("vector", "dv", d_py)
        for j in range(KT):
            ssl = j % 2
            if j >= 2:
                pl.wait("scalar", "dv", d_op_hist[j - 2])

            def silus(a, j=j, ssl=ssl):
                a.activation(uep[:, ssl * CH:(ssl + 1) * CH],
                             xsl[:, j * CH:(j + 1) * CH], AF.Silu)
                return a.activation(srb[:, ssl * CH:(ssl + 1) * CH],
                                    rl[:, j * CH:(j + 1) * CH], AF.Silu)
            pl.op("scalar", silus)
            a_s = fence_act()
            a_ep_last = a_s
            pl.wait("vector", "ac", a_s)

            def mk_op(v, j=j, ssl=ssl):
                v.scalar_tensor_tensor(
                    epsc[:, ssl * CH:(ssl + 1) * CH],
                    uep[:, ssl * CH:(ssl + 1) * CH], dvec[:, j:j + 1],
                    yall[:, j * CH:(j + 1) * CH], OP.mult, OP.add)
                return v.tensor_tensor(op16[:, j * CH:(j + 1) * CH],
                                       epsc[:, ssl * CH:(ssl + 1) * CH],
                                       srb[:, ssl * CH:(ssl + 1) * CH],
                                       OP.mult)
            d_op_hist[j] = pl.op("vector", mk_op, "dv")
        d_op = fence_dv()
        for m in range(KT):
            wsl = m % 2
            key = c * KT + m
            if key >= 2:
                pl.wait("gpsimd", "peg", wol_used[key - 2])
            l_w = pl.dma("gpsimd", "lw",
                         wol_r[:, wsl * KT:(wsl + 1) * KT, :], wout_t[m])
            pl.wait("tensor", "lw", l_w)
            if m == 0:
                pl.wait("tensor", "dv", d_op)
            bk = key % 2
            pl.pe_use(bk)

            def mm_out(t, wsl=wsl, bk=bk):
                for kt in range(KT):
                    ins = t.matmul(
                        bank[bk][:, :],
                        wol[:, (wsl * KT + kt) * P:(wsl * KT + kt + 1) * P],
                        op16[:, kt * CH:(kt + 1) * CH],
                        start=(kt == 0), stop=(kt == KT - 1))
                return ins
            gw = pl.op("tensor", mm_out, "peg")
            wol_used[key] = gw
            pl.wait("vector", "peg", gw)
            osl = key % 2
            if key >= 2:
                pl.wait("vector", "so", ost_used[key - 2])
            ev1 = pl.inc("evg")
            pl.rec("I", "vector", "evg", 1)
            h_evg5 = pl.h["evg"]

            def fin(v, m=m, bk=bk, osl=osl):
                ins = v.tensor_tensor(epsc[:, osl * CH:(osl + 1) * CH],
                                      bank[bk][:, :],
                                      xsl[:, m * CH:(m + 1) * CH], OP.add)
                ins.then_inc(h_evg5, 1)
                v.tensor_tensor(epsc[:, osl * CH:(osl + 1) * CH],
                                epsc[:, osl * CH:(osl + 1) * CH],
                                gl[:, m * CH:(m + 1) * CH], OP.mult)
                return v.tensor_tensor(ostg[:, osl * CH:(osl + 1) * CH],
                                       epsc[:, osl * CH:(osl + 1) * CH],
                                       xl[:, m * CH:(m + 1) * CH], OP.add)
            pl.op("vector", fin, "dv")
            d_f = fence_dv()
            d_f_last = d_f
            pl.bank_done(ev1, bk)
            pl.wait("sync", "dv", d_f)
            ost_used[key] = pl.dma(
                "sync", "so",
                out_t[m * P:(m + 1) * P, c * CH:(c + 1) * CH],
                ostg[:, osl * CH:(osl + 1) * CH])

    pl.wait("sync", "so", pl.val("so"))
    es_p6.close()

    pl.lag_waits()
    pl.simulate()

    with es:
        with nc.Block() as block:
            @block.sync
            def _(eng):
                for fn in pl.items["sync"]:
                    fn(eng)

            @block.vector
            def _(eng):
                for fn in pl.items["vector"]:
                    fn(eng)

            @block.scalar
            def _(eng):
                for fn in pl.items["scalar"]:
                    fn(eng)

            @block.tensor
            def _(eng):
                for fn in pl.items["tensor"]:
                    fn(eng)

            @block.gpsimd
            def _(eng):
                for fn in pl.items["gpsimd"]:
                    fn(eng)
    return nc



_NC_CACHE = [None]


def _run(inputs):
    in_maps, perms = _stage(inputs)
    if _NC_CACHE[0] is None:
        _NC_CACHE[0] = build_program()
    nc = _NC_CACHE[0]
    res = run_bass_kernel_spmd(nc, in_maps, core_ids=list(range(8)))
    return res, perms


def kernel(**inputs):
    res, perms = _run(inputs)
    out = np.empty((B_, L, DM), np.float32)
    for c in range(8):
        b, par = c // 2, c % 2
        o = np.asarray(res.results[c]["out_t"], np.float32)
        out[b, par * TOK:(par + 1) * TOK][:, perms[par]] = o.T
    return out


# version 04
# speedup vs baseline: 1.0004x; 1.0004x over previous
import contextlib
import numpy as np
import ml_dtypes
import concourse.bass as bass
import concourse.mybir as mybir
from concourse.bass_utils import run_bass_kernel_spmd

F32 = mybir.dt.float32
F16 = mybir.dt.float16
I16 = mybir.dt.int16
U8 = mybir.dt.uint8
AF = mybir.ActivationFunctionType
OP = mybir.AluOpType

P = 128
B_, L, DM = 4, 4096, 2048
TOK = 2048
CH = 512
NCH = 4
SC = 8
KT = 16
MLP = 16384
NST = 16
DTR = 128
DTMY = 8
NG_A = 32
NG_B = 16
EPS = 1e-5
SCL14 = 16384.0
RG = [[0, 1], [2, 3], [4, 5], [6, 7]]
DEBUG = False



def _stage(inp):
    ev = np.arange(0, DM, 2)
    od = np.arange(1, DM, 2)
    perms = [np.concatenate([ev, od]), np.concatenate([od, ev])]
    pN = np.concatenate([np.arange(0, MLP, 2), np.arange(1, MLP, 2)])

    x = np.asarray(inp["L0_attn_x"], np.float32)
    w_in = np.asarray(inp["L0_attn_W_in"], np.float32)
    conv_w = np.asarray(inp["L0_attn_conv_w"], np.float32)
    conv_b = np.asarray(inp["L0_attn_conv_b"], np.float32)
    w_x = np.asarray(inp["L0_attn_W_x"], np.float32)
    w_dt = np.asarray(inp["L0_attn_W_dt"], np.float32)
    b_dt = np.asarray(inp["L0_attn_b_dt"], np.float32)
    a_log = np.asarray(inp["L0_attn_A_log"], np.float32)
    dvec_ = np.asarray(inp["L0_attn_D"], np.float32)
    w_out = np.asarray(inp["L0_attn_W_out"], np.float32)

    oh = np.zeros((32, 32 * P), np.float32)
    for n in range(32):
        oh[n, n * P:(n + 1) * P] = 1.0
    e8 = np.zeros((8, P), np.float32)
    for j in range(8):
        e8[j, 16 * j:16 * (j + 1)] = SCL14

    shared = {}
    for par in (0, 1):
        pm = perms[par]
        cp = np.concatenate([pm, DM + pm])
        d = {}
        d["w_in_t"] = np.ascontiguousarray(
            w_in[pm][:, cp].reshape(KT, P, 32, P).transpose(2, 1, 0, 3))
        d["convw_t"] = np.ascontiguousarray(
            conv_w[pm][:, pm, :].transpose(2, 1, 0)
            .reshape(4, KT, P, KT, P).transpose(0, 3, 2, 1, 4))
        d["wx_t"] = np.ascontiguousarray(
            w_x[pm].reshape(KT, P, 160).transpose(1, 0, 2))
        d["wdt_t"] = np.ascontiguousarray(w_dt[:, pm[:1024]])
        d["wout_t"] = np.ascontiguousarray(
            w_out[pm][:, pm].reshape(KT, P, KT, P).transpose(2, 1, 0, 3))
        d["cvec_t"] = np.ascontiguousarray(conv_b[pm].reshape(KT, P).T)
        d["dvec_t"] = np.ascontiguousarray(dvec_[pm].reshape(KT, P).T)
        d["bdt_t"] = np.ascontiguousarray(b_dt[pm].reshape(KT, P).T)
        d["alog_t"] = np.ascontiguousarray(
            a_log[pm[:1024]].reshape(DTMY, P, NST).transpose(1, 0, 2)
            .reshape(P, DTMY * NST))
        for nm in ("gate", "up"):
            w = (np.asarray(inp[f"L0_ffn_{nm}_W_packed"]) & 0xFF
                 ).astype(np.uint8)[pN]
            if par == 1:
                w = (((w >> 4) | (w << 4)) & 0xFF).astype(np.uint8)
            wt = np.ascontiguousarray(w.T)
            d[f"{nm}_pk"] = np.ascontiguousarray(
                wt.reshape(8, P, NG_A, CH).transpose(2, 1, 0, 3))
            sc = np.asarray(inp[f"L0_ffn_{nm}_scales"], np.float32)[pN]
            d[f"{nm}_sc"] = np.ascontiguousarray(sc.T)
        wdp = (np.asarray(inp["L0_ffn_down_W_packed"]) & 0xFF
               ).astype(np.uint8)[pm]
        d["down_pk"] = np.ascontiguousarray(wdp.T.reshape(64, P, DM))
        d["down_sc"] = np.ascontiguousarray(
            np.asarray(inp["L0_ffn_down_scales"], np.float32)[pm].T)
        d["e8_t"] = e8
        d["oh_t"] = oh
        shared[par] = d

    in_maps = []
    for c in range(8):
        b, par = c // 2, c % 2
        pm = perms[par]
        m = dict(shared[par])
        xt = x[b, par * TOK:(par + 1) * TOK][:, pm]
        m["x_t"] = np.ascontiguousarray(xt.T)
        halo = np.zeros((DM, 4), np.float32)
        if par == 1:
            halo[:, 1:4] = x[b, TOK - 3:TOK][:, pm].T
        m["xhalo_t"] = np.ascontiguousarray(
            halo.reshape(KT, P, 4).transpose(1, 0, 2).reshape(P, KT * 4))
        in_maps.append(m)
    return in_maps, perms



class Plan:
    ENG = ("sync", "vector", "scalar", "tensor", "gpsimd")

    def __init__(self, nc):
        self.nc = nc
        self.items = {e: [] for e in self.ENG}
        self.cnt = {}
        self.h = {}
        self.bank_rel = [0] * 8
        self.trace = []
        self.wvals = []
        self.tagno = 0

    def rec(self, kind, e, sem, v):
        self.tagno += 1
        self.trace.append((kind, e, sem, v, self.tagno))

    def simulate(self, raise_=True):
        per = {e: [t for t in self.trace if t[1] == e] for e in self.ENG}
        ptr = {e: 0 for e in self.ENG}
        cnt = {s: 0 for s in self.cnt}
        progress = True
        while progress:
            progress = False
            for e in self.ENG:
                while ptr[e] < len(per[e]):
                    kind, _, sem, v, tag = per[e][ptr[e]]
                    if kind == "I":
                        cnt[sem] += v
                        ptr[e] += 1
                        progress = True
                    elif cnt[sem] >= self.wvals[v]:
                        ptr[e] += 1
                        progress = True
                    else:
                        break
        stuck = [per[e][ptr[e]] for e in self.ENG if ptr[e] < len(per[e])]
        if stuck and raise_:
            for t in stuck:
                print(f"DEADLOCK {t[1]}: waiting {t[2]} >= "
                      f"{self.wvals[t[3]]} (have {cnt[t[2]]}) tag={t[4]}")
            raise RuntimeError("plan deadlock")
        return stuck

    def lag_waits(self):
        import bisect
        pos = {s: [] for s in self.cnt}
        cum = {s: 0 for s in self.cnt}
        for t in self.trace:
            if t[0] == "I":
                cum[t[2]] += t[3]
                pos[t[2]].append(cum[t[2]])
        lagged = {}
        for t in self.trace:
            if t[0] != "W" or t[2] not in ("ac", "dv", "evg"):
                continue
            idx = t[3]
            v = self.wvals[idx]
            lst = pos[t[2]]
            j = bisect.bisect_left(lst, v)
            if j + 1 < len(lst):
                lagged[idx] = v
                self.wvals[idx] = lst[j + 1]
        per = {e: [t for t in self.trace if t[1] == e] for e in self.ENG}
        ptr = {e: 0 for e in self.ENG}
        cnt = {s: 0 for s in self.cnt}
        nrev = 0
        while True:
            progress = True
            while progress:
                progress = False
                for e in self.ENG:
                    while ptr[e] < len(per[e]):
                        kind, _, sem, v, tag = per[e][ptr[e]]
                        if kind == "I":
                            cnt[sem] += v
                            ptr[e] += 1
                            progress = True
                        elif cnt[sem] >= self.wvals[v]:
                            ptr[e] += 1
                            progress = True
                        else:
                            break
            stuck = [per[e][ptr[e]] for e in self.ENG if ptr[e] < len(per[e])]
            if not stuck:
                break
            hit = False
            for t in stuck:
                if t[0] == "W" and t[3] in lagged:
                    self.wvals[t[3]] = lagged.pop(t[3])
                    nrev += 1
                    hit = True
            if not hit:
                for t in stuck:
                    print(f"DEADLOCK {t[1]}: {t[2]} >= {self.wvals[t[3]]} "
                          f"(have {cnt[t[2]]}) tag={t[4]}")
                raise RuntimeError("plan deadlock (unlagged)")
        print(f"lag_waits: {len(lagged)} lagged, {nrev} reverted")

    def sem(self, name, handle):
        self.cnt[name] = 0
        self.h[name] = handle

    def inc(self, name, v=1):
        self.cnt[name] += v
        return self.cnt[name]

    def val(self, name):
        return self.cnt[name]

    def emit(self, e, fn):
        self.items[e].append(fn)

    def wait(self, e, name, v):
        if v <= 0:
            return
        assert v <= self.cnt[name], (e, name, v, self.cnt[name])
        idx = len(self.wvals)
        self.wvals.append(v)
        self.rec("W", e, name, idx)
        h = self.h[name]
        self.emit(e, lambda eng, h=h, i=idx: eng.wait_ge(h, self.wvals[i]))

    def op(self, e, fn, sem=None, v=1):
        if sem is None:
            self.emit(e, lambda eng, fn=fn: fn(eng))
            return None
        cnt = self.inc(sem, v)
        self.rec("I", e, sem, v)
        h = self.h[sem]

        def go(eng, fn=fn, h=h, v=v):
            fn(eng).then_inc(h, v)
        self.emit(e, go)
        return cnt

    def dma(self, e, name, out, in_):
        cnt = self.inc(name, 16)
        self.rec("I", e, name, 16)
        h = self.h[name]
        self.emit(e, lambda eng, h=h, o=out, i=in_:
                  eng.dma_start(out=o, in_=i).then_inc(h, 16))
        return cnt

    def _conds(self, eng):
        key = eng.engine
        if not hasattr(self, "_cond_cache"):
            self._cond_cache = {}
        if key not in self._cond_cache:
            pid = self.nc.partition_id(engines=[key])
            par = self.nc.snap(pid % 2)
            self._cond_cache[key] = (self.nc.snap(par == 0),
                                     self.nc.snap(par == 1))
        return self._cond_cache[key]

    def dma_pair(self, e, name, pairs):
        cnt = self.inc(name, 32 * len(pairs))
        self.rec("I", e, name, 32 * len(pairs))
        h = self.h[name]

        def go(eng, pairs=pairs, h=h):
            c_ev, c_od = self._conds(eng)
            for oe, ie, oo, io in pairs:
                eng.dma_start(out=oe, in_=ie, cond=c_ev).then_inc(h, 16)
                eng.dma_start(out=oo, in_=io, cond=c_od).then_inc(h, 16)
        self.emit(e, go)
        return cnt

    def pe_use(self, *banks):
        v = max(self.bank_rel[b] for b in banks)
        if v > 0:
            self.wait("tensor", "evg", v)

    def bank_done(self, ev, *banks):
        for b in banks:
            self.bank_rel[b] = ev

    def evac(self, e, fn, banks, extra_sem=None):
        ev = self.op(e, fn, "evg")
        self.bank_done(ev, *banks)
        if extra_sem is not None:
            return ev, None
        return ev



def build_program():
    nc = bass.Bass("TRN2", target_bir_lowering=False)

    def din(name, shape, dt=F32):
        return nc.dram_tensor(name, shape, dt, kind="ExternalInput").ap()

    x_t = din("x_t", [DM, TOK])
    xhalo_t = din("xhalo_t", [P, KT * 4])
    w_in_t = din("w_in_t", [32, P, KT, P])
    convw_t = din("convw_t", [4, KT, P, KT, P])
    wx_t = din("wx_t", [P, KT, 160])
    wdt_t = din("wdt_t", [DTR, 1024])
    wout_t = din("wout_t", [KT, P, KT, P])
    cvec_t = din("cvec_t", [P, KT])
    dvec_t = din("dvec_t", [P, KT])
    bdt_t = din("bdt_t", [P, KT])
    alog_t = din("alog_t", [P, DTMY * NST])
    gate_pk = din("gate_pk", [NG_A, P, 8, CH], U8)
    up_pk = din("up_pk", [NG_A, P, 8, CH], U8)
    down_pk = din("down_pk", [64, P, DM], U8)
    gate_sc = din("gate_sc", [64, MLP])
    up_sc = din("up_sc", [64, MLP])
    down_sc = din("down_sc", [512, DM])
    e8_t = din("e8_t", [8, P])
    oh_t = din("oh_t", [32, 32 * P])

    out_t = nc.dram_tensor("out_t", [DM, TOK], F32, kind="ExternalOutput").ap()

    dkind = dict(kind="ExternalOutput") if DEBUG else {}
    xs_d = nc.dram_tensor("xs_d", [DM, TOK], F32).ap()
    r_d = nc.dram_tensor("r_d", [DM, TOK], F16, **dkind).ap()
    int_d = nc.dram_tensor("int_d", [MLP, TOK], F16, **dkind).ap()
    g_d = nc.dram_tensor("g_d", [DM, TOK], F16, **dkind).ap()
    y_d = nc.dram_tensor("y_d", [1024, L], F32).ap()
    xsc_d = nc.dram_tensor("xsc_d", [1024, L], F32, **dkind).ap()
    cc1i = nc.dram_tensor("cc1i", [160, TOK], F32).ap()
    cc1o = nc.dram_tensor("cc1o", [160, TOK], F32).ap()
    cc2i = nc.dram_tensor("cc2i", [1024, TOK], F32).ap()
    cc2o = nc.dram_tensor("cc2o", [1024, TOK], F32).ap()
    cc3o = nc.dram_tensor("cc3o", [1024, L], F32).ap()
    dbg_d = nc.dram_tensor("dbg_d", [DTR, L], F32).ap()
    bcg_d = nc.dram_tensor("bcg_d", [32, L], F32).ap()
    pxs_d = nc.dram_tensor("pxs_d", [1024, TOK], F32).ap()
    ymg_d = nc.dram_tensor("ymg_d", [DM, TOK], F32).ap()

    es = contextlib.ExitStack()
    sb = lambda n, shp, dt: es.enter_context(nc.sbuf_tensor(n, shp, dt))

    pl = Plan(nc)
    for s in ["lx", "lw", "lp", "ls", "li", "le", "lc",
              "peg", "evg", "dv", "ac", "po",
              "sx", "sr", "si", "sy", "sm", "so", "cc"]:
        pl.sem(s, es.enter_context(nc.semaphore(s)))

    bank = [es.enter_context(nc.psum_tensor(f"pb{i}", [P, CH], F32))
            for i in range(8)]

    ones_col = sb("ones_col", [P, 1], F16)
    ones_row = sb("ones_row", [1, P], F16)
    e8_sb = sb("e8_sb", [8, P], F16)
    cvec = sb("cvec", [P, KT], F32)
    dvec = sb("dvec", [P, KT], F32)
    bdt = sb("bdt", [P, KT], F32)
    aneg = sb("aneg", [P, DTMY * NST], F32)
    wdt16 = sb("wdt16", [DTR, 1024], F16)
    carry = sb("carry", [P, NST * DTMY], F32)
    strow = sb("strow", [1, 3 * CH], F32)
    strow16 = sb("strow16", [1, 2 * CH], F16)
    drsrc = sb("drsrc", [P, P], F16)

    pl.op("vector", lambda v: v.memset(ones_col[:, :], 1.0))
    pl.op("vector", lambda v: v.memset(ones_row[:, :], 1.0))
    pl.op("vector", lambda v: v.memset(drsrc[:, :], 0.0))
    pl.op("vector", lambda v: v.memset(strow[:, :], 0.0))
    pl.op("vector", lambda v: v.memset(strow16[:, :], 0.0))
    d_ones = pl.op("vector", lambda v: v.memset(carry[:, :], 0.0), "dv")
    pl.op("vector", lambda v: v.tensor_tensor_scan(
        strow[0:1, 2 * CH:2 * CH + 16], strow[0:1, 0:16], strow[0:1, 16:32],
        0.0, OP.mult, OP.add))
    pl.op("vector", lambda v: v.scalar_tensor_tensor(
        strow[0:1, 2 * CH:2 * CH + 16], strow[0:1, 0:16], 1.0,
        strow[0:1, 16:32], OP.mult, OP.add))
    pl.op("vector", lambda v: v.tensor_tensor_scan(
        strow[0:1, 2 * CH:2 * CH + 16], strow[0:1, 0:16], strow[0:1, 16:32],
        0.0, OP.mult, OP.add))

    def fence_act(sem="ac", v=1):
        return pl.op("scalar", lambda a: a.activation(
            strow[0:1, 8:9], strow[0:1, 9:10], AF.Copy), sem, v)

    def fence_dv(sem="dv", v=1):
        return pl.op("vector", lambda vv: vv.tensor_copy(
            strow[0:1, 2 * CH:2 * CH + 256], strow[0:1, CH:CH + 256]), sem, v)

    def drain_mm(t):
        return t.matmul(bank[5][0:1, 0:P], ones_col[:, :1], drsrc[:, :],
                        start=True, stop=True)
    l_e8 = pl.dma("gpsimd", "lw", e8_sb[:, :], e8_t)
    pl.dma("sync", "lc", cvec[:, :], cvec_t)
    pl.dma("sync", "lc", dvec[:, :], dvec_t)
    pl.dma("sync", "lc", bdt[:, :], bdt_t)
    pl.dma("sync", "lc", aneg[:, :], alog_t)
    l_wdt = pl.dma("gpsimd", "lw", wdt16[:, :], wdt_t)
    pl.wait("scalar", "lc", pl.val("lc"))
    a_exp = pl.op("scalar", lambda a: a.activation(
        aneg[:, :], aneg[:, :], AF.Exp), "ac")
    pl.wait("vector", "ac", a_exp)
    pl.wait("vector", "lc", pl.val("lc"))
    d_aneg = pl.op("vector", lambda v: v.tensor_scalar(
        aneg[:, :], aneg[:, :], -1.0, None, OP.mult), "dv")

    es_xn = contextlib.ExitStack()
    xn = es_xn.enter_context(nc.sbuf_tensor("xn", [P, KT * TOK], F16))

    es_p0 = contextlib.ExitStack()
    sb0 = lambda n, shp, dt: es_p0.enter_context(nc.sbuf_tensor(n, shp, dt))
    xh = sb0("xh", [P, KT * CH], F16)
    xsq = sb0("xsq", [P, KT * CH], F16)
    xnh = sb0("xnh", [P, KT * 4], F16)
    lnsc = sb0("lnsc", [P, CH], F32)

    x_r = x_t.rearrange("(t p) n -> t p n", p=P)
    xh_r = xh.ap().rearrange("p (t n) -> p t n", n=CH)

    ln_state = {"g_sums": 0, "a_rows": 0}

    def ln_block(src, width, dst, dst_off, dst_stride, sq_src):
        pl.wait("scalar", "peg", ln_state["g_sums"])
        pl.op("scalar", lambda a, src=src, sq=sq_src: a.activation(
            sq[:, :], src[:, :], AF.Square))
        a_sq = fence_act()
        pl.wait("tensor", "ac", a_sq)
        pl.pe_use(0, 1)

        def mm_sums(t, src=src, sq=sq_src, width=width):
            for kt in range(KT):
                t.matmul(bank[0][0:1, 0:width], ones_col[:, :1],
                         src[:, kt * width:(kt + 1) * width],
                         start=(kt == 0), stop=(kt == KT - 1))
            for kt in range(KT):
                ins = t.matmul(bank[1][0:1, 0:width], ones_col[:, :1],
                               sq[:, kt * width:(kt + 1) * width],
                               start=(kt == 0), stop=(kt == KT - 1))
            if width < 128:
                ins = drain_mm(t)
            return ins
        g_sums = pl.op("tensor", mm_sums, "peg")
        ln_state["g_sums"] = g_sums
        pl.wait("vector", "peg", g_sums)
        pl.wait("vector", "ac", ln_state["a_rows"])

        ev1 = pl.inc("evg")
        pl.rec("I", "vector", "evg", 1)
        h_evg = pl.h["evg"]

        def st1(v, width=width):
            v.tensor_scalar(strow[:1, 0:width], bank[0][0:1, 0:width],
                            1.0 / DM, None, OP.mult)
            ins = v.tensor_scalar(strow[:1, CH:CH + width],
                                  bank[1][0:1, 0:width],
                                  1.0 / DM, None, OP.mult)
            ins.then_inc(h_evg, 1)
            v.tensor_tensor(strow[:1, 2 * CH:2 * CH + width],
                            strow[:1, 0:width], strow[:1, 0:width], OP.mult)
            v.tensor_tensor(strow[:1, CH:CH + width],
                            strow[:1, CH:CH + width],
                            strow[:1, 2 * CH:2 * CH + width],
                            OP.subtract)
            return v.tensor_scalar(strow[:1, CH:CH + width],
                                   strow[:1, CH:CH + width], EPS, None,
                                   OP.add)
        pl.op("vector", st1)
        d_var = fence_dv()
        pl.bank_done(ev1, 0, 1)
        pl.wait("scalar", "dv", d_var)

        def st2(a, width=width):
            a.activation(strow[:1, CH:CH + width], strow[:1, CH:CH + width],
                         AF.Ln)
            a.activation(strow[:1, CH:CH + width], strow[:1, CH:CH + width],
                         AF.Exp, scale=-0.5)
            a.activation(strow16[:1, 0:width], strow[:1, 0:width], AF.Copy)
            return a.activation(strow16[:1, CH:CH + width],
                                strow[:1, CH:CH + width], AF.Copy)
        pl.op("scalar", st2)
        a_rows = fence_act()
        ln_state["a_rows"] = a_rows
        pl.wait("tensor", "ac", a_rows)
        pl.pe_use(2, 3)

        def mm_bc(t, width=width):
            t.matmul(bank[2][:, 0:width], ones_row[:1, :],
                     strow16[:1, 0:width], start=True, stop=True)
            ins = t.matmul(bank[3][:, 0:width], ones_row[:1, :],
                           strow16[:1, CH:CH + width], start=True, stop=True)
            if width < 128:
                ins = drain_mm(t)
            return ins
        g_bc = pl.op("tensor", mm_bc, "peg")
        pl.wait("vector", "peg", g_bc)

        ev2 = pl.inc("evg")
        pl.rec("I", "vector", "evg", 1)

        def apply_ln(v, src=src, dst=dst, off=dst_off, stride=dst_stride,
                     width=width):
            for kt in range(KT):
                ins = v.tensor_tensor(lnsc[:, 0:width],
                                      src[:, kt * width:(kt + 1) * width],
                                      bank[2][:, 0:width], OP.subtract)
                if kt == KT - 1:
                    ins.then_inc(h_evg, 1)
                ins2 = v.tensor_tensor(
                    dst[:, off + kt * stride:off + kt * stride + width],
                    lnsc[:, 0:width], bank[3][:, 0:width], OP.mult)
            return ins2
        pl.op("vector", apply_ln)
        d_xn = fence_dv()
        pl.bank_done(ev2, 2, 3)
        return d_xn

    pl.dma("gpsimd", "lx", xh_r, x_r[:, :, 0:CH].transpose([1, 0, 2]))
    l_hal = pl.dma("gpsimd", "lx", xh_r[:, :, 0:4],
                   xhalo_t.rearrange("p (t n) -> p t n", n=4))
    pl.wait("scalar", "lx", l_hal)
    pl.wait("vector", "lx", l_hal)
    pl.wait("tensor", "lx", l_hal)
    pl.wait("tensor", "dv", d_ones)
    d_halo = ln_block(xh.ap(), CH, xh.ap(), 0, CH, xsq.ap())

    def compact_halo(v):
        return v.tensor_copy(
            xnh.ap().rearrange("p (t n) -> p t n", n=4),
            xh.ap().rearrange("p (t n) -> p t n", n=CH)[:, :, 0:4])
    d_halo = pl.op("vector", compact_halo, "dv")

    xn_applied = []
    for c in range(NCH):
        if c > 0:
            pl.wait("gpsimd", "dv", xn_applied[c - 1])
        else:
            pl.wait("gpsimd", "dv", d_halo)
        l_c = pl.dma("gpsimd", "lx", xh_r,
                     x_r[:, :, c * CH:(c + 1) * CH].transpose([1, 0, 2]))
        pl.wait("scalar", "lx", l_c)
        pl.wait("vector", "lx", l_c)
        pl.wait("tensor", "lx", l_c)
        xn_applied.append(
            ln_block(xh.ap(), CH, xn.ap(), c * CH, TOK, xsq.ap()))

    es_dbc = contextlib.ExitStack()
    dbc_d = es_dbc.enter_context(
        nc.sbuf_tensor("dbc_d", [DTR, TOK], F32, side="right"))
    bc_d = es_dbc.enter_context(
        nc.sbuf_tensor("bc_d", [32, TOK], F32, side="right"))
    es_xc = contextlib.ExitStack()
    xc = es_xc.enter_context(
        nc.sbuf_tensor("xc", [P, KT * 2052], F16, side="right"))

    es_p1 = contextlib.ExitStack()
    wl = es_p1.enter_context(nc.sbuf_tensor("wl", [P, 2 * KT * P], F16))
    rstg = es_p1.enter_context(nc.sbuf_tensor("rstg", [P, 2 * CH], F16))

    pl.wait("tensor", "dv", xn_applied[NCH - 1])
    wl_r = wl.ap().rearrange("p (t n) -> p t n", n=P)
    wl_used = {}
    rst_used = {}
    ev_xc_last = 0
    for m in range(32):
        slot = m % 2
        if m >= 2:
            pl.wait("gpsimd", "peg", wl_used[m - 2])
        l_w = pl.dma("gpsimd", "lw",
                     wl_r[:, slot * KT:(slot + 1) * KT, :], w_in_t[m])
        pl.wait("tensor", "lw", l_w)
        for c in range(NCH):
            bk = (m * NCH + c) % 2
            pl.pe_use(bk)

            def mm_xz(t, c=c, slot=slot, bk=bk):
                for kt in range(KT):
                    ins = t.matmul(
                        bank[bk][:, :],
                        wl[:, (slot * KT + kt) * P:(slot * KT + kt + 1) * P],
                        xn[:, kt * TOK + c * CH:kt * TOK + (c + 1) * CH],
                        start=(kt == 0), stop=(kt == KT - 1))
                return ins
            g = pl.op("tensor", mm_xz, "peg")
            pl.wait("vector", "peg", g)
            if m < KT:
                ev = pl.evac("vector",
                             lambda v, m=m, c=c, bk=bk: v.tensor_copy(
                                 xc[:, m * 2052 + 4 + c * CH:
                                    m * 2052 + 4 + (c + 1) * CH],
                                 bank[bk][:, :]), [bk])
                ev_xc_last = ev
            else:
                rs = (m * NCH + c) % 2
                ridx = (m - KT) * NCH + c
                if ridx >= 2:
                    pl.wait("vector", "sr", rst_used[ridx - 2])
                ev = pl.evac("vector",
                             lambda v, bk=bk, rs=rs: v.tensor_copy(
                                 rstg[:, rs * CH:(rs + 1) * CH],
                                 bank[bk][:, :]), [bk])
                pl.wait("sync", "evg", ev)
                rst_used[ridx] = pl.dma(
                    "sync", "sr",
                    r_d[(m - KT) * P:(m - KT + 1) * P, c * CH:(c + 1) * CH],
                    rstg[:, rs * CH:(rs + 1) * CH])
        if m < KT:
            pl.pe_use(4)

            def mm_halo(t, slot=slot):
                for kt in range(KT):
                    t.matmul(
                        bank[4][:, 0:4],
                        wl[:, (slot * KT + kt) * P:(slot * KT + kt + 1) * P],
                        xnh[:, kt * 4:(kt + 1) * 4],
                        start=(kt == 0), stop=(kt == KT - 1))
                return drain_mm(t)
            g2 = pl.op("tensor", mm_halo, "peg")
            wl_used[m] = g2
            pl.wait("vector", "peg", g2)
            ev = pl.evac("vector", lambda v, m=m: v.tensor_copy(
                xc[:, m * 2052:m * 2052 + 4], bank[4][:, 0:4]), [4])
            ev_xc_last = ev
        else:
            wl_used[m] = g

    es_p1.close()
    es_p0.close()

    if DEBUG:
        xhl_dbg = nc.dram_tensor("xhl_dbg", [P, KT * 4], F16,
                                 kind="ExternalOutput").ap()
        st_dbg = nc.dram_tensor("st_dbg", [1, 3 * CH], F32,
                                kind="ExternalOutput").ap()
        st16_dbg = nc.dram_tensor("st16_dbg", [1, 2 * CH], F16,
                                  kind="ExternalOutput").ap()
        pl.wait("sync", "dv", d_halo)
        pl.dma("sync", "sm", xhl_dbg, xnh[:, :])
        pl.dma("sync", "sm", st_dbg, strow[:1, :])
        pl.dma("sync", "sm", st16_dbg, strow16[:1, :])
        xch_dbg = nc.dram_tensor("xch_dbg", [P, KT * 8], F16,
                                 kind="ExternalOutput").ap()
        pl.wait("sync", "evg", ev_xc_last)
        pl.dma("sync", "sm", xch_dbg,
               xc.ap().rearrange("p (t n) -> p t n", n=2052)[:, :, 0:8])
        xnh_dbg = nc.dram_tensor("xnh_dbg", [P, KT * 4], F16,
                                 kind="ExternalOutput").ap()
        pl.dma("sync", "sm", xnh_dbg, xnh[:, :])
        xn_dbg = nc.dram_tensor("xn_dbg", [DM, TOK], F16,
                                kind="ExternalOutput").ap()
        pl.wait("sync", "dv", xn_applied[NCH - 1])
        pl.dma("sync", "sm",
               xn_dbg.rearrange("(t p) n -> t p n", p=P).transpose([1, 0, 2]),
               xn.ap().rearrange("p (t n) -> p t n", n=TOK))

    es_p2 = contextlib.ExitStack()
    cw = es_p2.enter_context(nc.sbuf_tensor("cw", [P, 2 * 4 * KT * P], F16))
    xstg = es_p2.enter_context(nc.sbuf_tensor("xstg", [P, 2 * CH], F32))

    ev_xcf = fence_dv()
    pl.wait("vector", "evg", 0) if False else None
    pl.wait("tensor", "dv", ev_xcf)
    pl.wait("tensor", "evg", ev_xc_last)
    pl.wait("gpsimd", "peg", wl_used[31])
    pl.wait("vector", "sr", pl.val("sr"))
    cw_r = cw.ap().rearrange("p (s k t n) -> p s k t n", s=2, k=4, n=P)
    cw_used = {}
    xst_used = {}
    for m in range(KT):
        slot = m % 2
        if m >= 2:
            pl.wait("gpsimd", "peg", cw_used[m - 2])
        for kk in range(4):
            l_cw = pl.dma("gpsimd", "lw", cw_r[:, slot, kk], convw_t[kk, m])
        pl.wait("tensor", "lw", l_cw)
        for c in range(NCH):
            bk = (m * NCH + c) % 2
            pl.pe_use(bk)

            def mm_conv(t, c=c, slot=slot, bk=bk):
                for kk in range(4):
                    for kt in range(KT):
                        ins = t.matmul(
                            bank[bk][:, :],
                            cw[:, ((slot * 4 + kk) * KT + kt) * P:
                               ((slot * 4 + kk) * KT + kt + 1) * P],
                            xc[:, kt * 2052 + c * CH + kk + 1:
                               kt * 2052 + c * CH + kk + 1 + CH],
                            start=(kk == 0 and kt == 0),
                            stop=(kk == 3 and kt == KT - 1))
                return ins
            g = pl.op("tensor", mm_conv, "peg")
            if c == NCH - 1:
                cw_used[m] = g
            idx = m * NCH + c
            xsl_ = idx % 2
            pl.wait("vector", "peg", g)
            if idx >= 2:
                pl.wait("vector", "sx", xst_used[idx - 2])
            ev = pl.evac("vector",
                         lambda v, bk=bk, m=m, xsl_=xsl_: v.tensor_scalar(
                             xstg[:, xsl_ * CH:(xsl_ + 1) * CH],
                             bank[bk][:, :], cvec[:, m:m + 1], None,
                             OP.add), [bk])
            pl.wait("sync", "evg", ev)
            xst_used[idx] = pl.dma(
                "sync", "sx",
                xs_d[m * P:(m + 1) * P, c * CH:(c + 1) * CH],
                xstg[:, xsl_ * CH:(xsl_ + 1) * CH])

    es_p2.close()
    es_xc.close()
    es_p2 = contextlib.ExitStack()
    wx16 = es_p2.enter_context(nc.sbuf_tensor("wx16", [P, KT * 160], F16))
    xsf = es_p2.enter_context(nc.sbuf_tensor("xsf", [P, KT * CH], F16))
    u16 = es_p2.enter_context(nc.sbuf_tensor("u16", [P, KT * CH], F16))

    xs_r = xs_d.rearrange("(t p) n -> t p n", p=P)
    pl.wait("gpsimd", "sx", pl.val("sx"))
    pl.wait("gpsimd", "peg", cw_used[KT - 1])
    l_wx = pl.dma("gpsimd", "lw",
                  wx16.ap().rearrange("p (t n) -> p t n", n=160), wx_t)
    a_u_prev = 0
    g_dbc_prev = 0
    for c in range(NCH):
        if c > 0:
            pl.wait("gpsimd", "ac", a_u_prev)
        l_u = pl.dma("gpsimd", "lx",
                     xsf.ap().rearrange("p (t n) -> p t n", n=CH),
                     xs_r[:, :, c * CH:(c + 1) * CH].transpose([1, 0, 2]))
        pl.wait("scalar", "lx", l_u)
        pl.wait("scalar", "peg", g_dbc_prev)
        pl.op("scalar", lambda a: a.activation(
            u16[:, :], xsf[:, :], AF.Silu))
        a_u = fence_act()
        a_u_prev = a_u
        pl.wait("tensor", "ac", a_u)
        pl.wait("tensor", "lw", l_wx)
        pl.pe_use(0, 2)

        def mm_dbc(t):
            for kt in range(KT):
                t.matmul(bank[0][:, :],
                         wx16[:, kt * 160:kt * 160 + 128],
                         u16[:, kt * CH:(kt + 1) * CH],
                         start=(kt == 0), stop=(kt == KT - 1))
            for kt in range(KT):
                ins = t.matmul(bank[2][0:32, :],
                               wx16[:, kt * 160 + 128:kt * 160 + 160],
                               u16[:, kt * CH:(kt + 1) * CH],
                               start=(kt == 0), stop=(kt == KT - 1))
            return ins
        g = pl.op("tensor", mm_dbc, "peg")
        g_dbc_prev = g
        pl.wait("vector", "peg", g)
        ev1 = pl.inc("evg")
        pl.rec("I", "vector", "evg", 1)
        h_evg = pl.h["evg"]

        def ev_dbc(v, c=c):
            ins = v.tensor_copy(dbc_d[:, c * CH:(c + 1) * CH], bank[0][:, :])
            ins.then_inc(h_evg, 1)
            return v.tensor_copy(bc_d[0:32, c * CH:(c + 1) * CH],
                                 bank[2][0:32, :])
        ev2 = pl.op("vector", ev_dbc, "evg")
        pl.bank_done(ev1, 0)
        pl.bank_done(ev2, 2)

    pl.wait("sync", "evg", pl.val("evg"))
    pl.dma("sync", "sm", cc1i[0:DTR, :], dbc_d[:, :])
    pl.dma("sync", "sm", cc1i[DTR:160, :], bc_d[0:32, :])
    pl.wait("sync", "sx", pl.val("sx"))
    pl.dma("sync", "sm", cc2i, xs_d[1024:2048, :])
    pl.wait("gpsimd", "sm", pl.val("sm"))
    c_cc1 = pl.op("gpsimd", lambda g: g.collective_compute(
        "AllReduce", OP.add, replica_groups=RG, ins=[cc1i], outs=[cc1o]),
        "cc")
    c_cc2 = pl.op("gpsimd", lambda g: g.collective_compute(
        "AllReduce", OP.add, replica_groups=RG, ins=[cc2i], outs=[cc2o]),
        "cc")
    for _e in Plan.ENG:
        pl.wait(_e, "cc", c_cc2)

    es_p2.close()

    es_p3 = contextlib.ExitStack()
    sb3 = lambda n, shp, dt: es_p3.enter_context(nc.sbuf_tensor(n, shp, dt))
    glhs = sb3("glhs", [P, 2 * KT * CH], F16)
    ulhs = sb3("ulhs", [P, 2 * KT * CH], F16)
    pkg = sb3("pkg", [P, 8 * CH], I16)
    pku = sb3("pku", [P, 8 * CH], I16)
    lo16 = sb3("lo16", [P, CH], I16)
    hi16 = sb3("hi16", [P, CH], I16)
    scg16 = sb3("scg16", [8, 8 * CH], F16)
    scu16 = sb3("scu16", [8, 8 * CH], F16)
    gel = sb3("gel", [P, 2 * CH], F16)
    istg = sb3("istg", [P, 2 * CH], F16)

    gsc_r = gate_sc.rearrange("(w r) n -> r w n", r=8)
    usc_r = up_sc.rearrange("(w r) n -> r w n", r=8)
    deq_done = {0 - 1: 0}
    lhs_used = {}
    ist_used = {}
    ist_ev = {}
    sc_mm_last = {-1: 0}
    pl.wait("vector", "lw", l_e8)
    pl.wait("tensor", "lw", l_e8)
    pl.wait("gpsimd", "peg", g_dbc_prev)
    pl.wait("gpsimd", "ac", a_u_prev)
    for g in range(NG_A):
        slot = g % 2
        pl.wait("gpsimd", "dv", deq_done[g - 1])
        l_pk = pl.dma("gpsimd", "lp",
                      pkg.ap().rearrange("p (a n) -> p a n", n=CH),
                      gate_pk[g])
        l_pk2 = pl.dma("gpsimd", "lp",
                       pku.ap().rearrange("p (a n) -> p a n", n=CH),
                       up_pk[g])
        pl.wait("gpsimd", "peg", sc_mm_last[g - 1])
        l_sc = pl.dma("gpsimd", "ls",
                      scg16.ap().rearrange("r (w n) -> r w n", n=CH),
                      gsc_r[:, :, g * CH:(g + 1) * CH])
        l_sc2 = pl.dma("gpsimd", "ls",
                       scu16.ap().rearrange("r (w n) -> r w n", n=CH),
                       usc_r[:, :, g * CH:(g + 1) * CH])
        pl.wait("tensor", "ls", l_sc2)
        pl.wait("vector", "lp", l_pk2)
        if g >= 2:
            pl.wait("vector", "peg", lhs_used[g - 2])
        for mat, pk, sct, lhs in ((0, pkg, scg16, glhs),
                                  (1, pku, scu16, ulhs)):
            for kw in range(8):
                bsc = 6 + (kw + mat * 8) % 2
                pl.pe_use(bsc)

                def mm_sc(t, sct=sct, kw=kw, bsc=bsc):
                    return t.matmul(bank[bsc][:, :], e8_sb[:, :],
                                    sct[:, kw * CH:(kw + 1) * CH],
                                    start=True, stop=True)
                gq = pl.op("tensor", mm_sc, "peg")
                if mat == 1 and kw == 7:
                    sc_mm_last[g] = gq
                pl.wait("vector", "peg", gq)

                def deq(v, pk=pk, lhs=lhs, kw=kw, slot=slot, bsc=bsc):
                    src = pk[:, kw * CH:(kw + 1) * CH]
                    v.tensor_scalar(lo16[:, :], src, 7, 9,
                                    OP.bitwise_and, OP.logical_shift_left)
                    v.tensor_scalar(hi16[:, :], src, 8, 12,
                                    OP.bitwise_and, OP.logical_shift_left)
                    v.tensor_tensor(lo16[:, :], lo16[:, :], hi16[:, :],
                                    OP.bitwise_or)
                    v.tensor_scalar(hi16[:, :], src, 112, 5,
                                    OP.bitwise_and, OP.logical_shift_left)
                    v.tensor_tensor(
                        lhs[:, (slot * KT + kw) * CH:
                            (slot * KT + kw + 1) * CH],
                        lo16[:, :].bitcast(F16), bank[bsc][:, :], OP.mult)
                    v.tensor_scalar(lo16[:, :], src, 128, 8,
                                    OP.bitwise_and, OP.logical_shift_left)
                    v.tensor_tensor(lo16[:, :], hi16[:, :], lo16[:, :],
                                    OP.bitwise_or)
                    return v.tensor_tensor(
                        lhs[:, (slot * KT + 8 + kw) * CH:
                            (slot * KT + 8 + kw + 1) * CH],
                        lo16[:, :].bitcast(F16), bank[bsc][:, :], OP.mult)
                ev = pl.evac("vector", deq, [bsc])
        h_dv = pl.h["dv"]
        deq_done[g] = pl.inc("dv")
        pl.rec("I", "vector", "dv", 1)
        pl.emit("vector", lambda v, h=h_dv: v.tensor_copy(
            lo16[:, 0:4], hi16[:, 0:4]).then_inc(h, 1))
        if DEBUG and g in (0, 30, 31):
            gt = nc.dram_tensor(f"glhs_dbg{g}", [P, KT * CH], F16,
                                kind="ExternalOutput").ap()
            ut = nc.dram_tensor(f"ulhs_dbg{g}", [P, KT * CH], F16,
                                kind="ExternalOutput").ap()
            pl.wait("sync", "dv", deq_done[g])
            pl.dma("sync", "sm", gt,
                   glhs[:, slot * KT * CH:(slot + 1) * KT * CH])
            s_dbgw = pl.dma("sync", "sm", ut,
                            ulhs[:, slot * KT * CH:(slot + 1) * KT * CH])
            pl.wait("vector", "sm", s_dbgw)

        pl.wait("tensor", "dv", deq_done[g])
        for mi in range(4):
            for c in range(NCH):
                bg = ((mi * NCH + c) % 2) * 2
                pl.pe_use(bg, bg + 1)

                def mm_gu(t, mi=mi, c=c, slot=slot, bg=bg):
                    for kt in range(KT):
                        t.matmul(bank[bg][:, :],
                                 glhs[:, (slot * KT + kt) * CH + mi * P:
                                      (slot * KT + kt) * CH + (mi + 1) * P],
                                 xn[:, kt * TOK + c * CH:
                                    kt * TOK + (c + 1) * CH],
                                 start=(kt == 0), stop=(kt == KT - 1))
                    for kt in range(KT):
                        ins = t.matmul(
                            bank[bg + 1][:, :],
                            ulhs[:, (slot * KT + kt) * CH + mi * P:
                                 (slot * KT + kt) * CH + (mi + 1) * P],
                            xn[:, kt * TOK + c * CH:kt * TOK + (c + 1) * CH],
                            start=(kt == 0), stop=(kt == KT - 1))
                    return ins
                gg = pl.op("tensor", mm_gu, "peg")
                if mi == 3 and c == NCH - 1:
                    lhs_used[g] = gg
                isl = (mi * NCH + c) % 2
                idx = (g * 4 + mi) * NCH + c
                pl.wait("scalar", "peg", gg)
                if idx >= 2:
                    pl.wait("scalar", "evg", ist_ev[idx - 2])
                pl.op("scalar", lambda a, bg=bg, isl=isl:
                      a.activation(gel[:, isl * CH:(isl + 1) * CH],
                                   bank[bg][:, :], AF.Gelu_apprx_tanh))
                a_gel = fence_act()
                pl.wait("vector", "ac", a_gel)
                if idx >= 2:
                    pl.wait("vector", "si", ist_used[idx - 2])

                def mul_iu(v, bg=bg, isl=isl):
                    return v.tensor_tensor(istg[:, isl * CH:(isl + 1) * CH],
                                           gel[:, isl * CH:(isl + 1) * CH],
                                           bank[bg + 1][:, :], OP.mult)
                ev2 = pl.op("vector", mul_iu, "evg")
                pl.bank_done(ev2, bg, bg + 1)
                ist_ev[idx] = ev2
                pl.wait("sync", "evg", ev2)
                it = g * 4 + mi
                ist_used[idx] = pl.dma(
                    "sync", "si",
                    int_d[it * P:(it + 1) * P, c * CH:(c + 1) * CH],
                    istg[:, isl * CH:(isl + 1) * CH])

    es_p3.close()
    es_xn.close()

    es_scan = contextlib.ExitStack()
    dbc16 = es_scan.enter_context(nc.sbuf_tensor("dbc16", [DTR, L], F16))
    bc16 = es_scan.enter_context(nc.sbuf_tensor("bc16", [32, L], F16))
    oh_sb = es_scan.enter_context(nc.sbuf_tensor("oh_sb", [32, 32 * P], F16))
    pl.wait("gpsimd", "peg", lhs_used[NG_A - 1])
    pl.wait("gpsimd", "si", pl.val("si"))
    l_oh = pl.dma("gpsimd", "lw", oh_sb[:, :], oh_t)

    es_x = contextlib.ExitStack()
    ta = es_x.enter_context(nc.sbuf_tensor("ta", [P, TOK], F32))
    tb = es_x.enter_context(nc.sbuf_tensor("tb", [P, TOK], F32))
    tcx = es_x.enter_context(nc.sbuf_tensor("tcx", [P, TOK], F32))

    pl.wait("sync", "cc", c_cc1)
    pl.wait("sync", "peg", lhs_used[NG_A - 1])
    pl.wait("sync", "si", pl.val("si"))
    pl.dma("sync", "le", ta[:, :], cc1o[0:DTR, :])
    l2 = pl.dma("sync", "le", tb[0:32, :], cc1o[DTR:160, :])
    pl.wait("vector", "le", l2)

    def sub1(v):
        v.tensor_sub(ta[:, :], ta[:, :], dbc_d[:, :])
        return v.tensor_sub(tb[0:32, :], tb[0:32, :], bc_d[0:32, :])
    d_s1 = pl.op("vector", sub1, "dv")
    pl.wait("sync", "dv", d_s1)
    s_db = pl.dma_pair("sync", "sm", [
        (dbg_d[:, 0:TOK], dbc_d[:, :], dbg_d[:, TOK:L], dbc_d[:, :]),
        (dbg_d[:, TOK:L], ta[:, :], dbg_d[:, 0:TOK], ta[:, :]),
        (bcg_d[0:32, 0:TOK], bc_d[0:32, :], bcg_d[0:32, TOK:L],
         bc_d[0:32, :]),
        (bcg_d[0:32, TOK:L], tb[0:32, :], bcg_d[0:32, 0:TOK], tb[0:32, :]),
    ])
    pl.wait("gpsimd", "sm", s_db)
    pl.dma("gpsimd", "le", dbc16[:, :], dbg_d)
    l_db2 = pl.dma("gpsimd", "le", bc16[0:32, :], bcg_d)
    h_dvdb = pl.h["dv"]
    d_db = pl.inc("dv")
    pl.wait("vector", "le", l_db2)
    pl.rec("I", "vector", "dv", 1)
    pl.emit("vector", lambda v, h=h_dvdb: v.tensor_copy(
        tcx[:, 0:256], ta[:, 0:256]).then_inc(h, 1))

    pl.wait("sync", "cc", c_cc2)
    d_sub_prev = 0
    sm_prev = 0
    for j in range(DTMY):
        if j > 0:
            pl.wait("sync", "dv", d_sub_prev)
        l3 = pl.dma("sync", "le", ta[:, :], cc2o[j * P:(j + 1) * P, :])
        l4 = pl.dma("sync", "le", tb[:, :],
                    xs_d[1024 + j * P:1024 + (j + 1) * P, :])
        pl.wait("vector", "le", l4)
        if j > 0:
            pl.wait("vector", "sm", sm_prev)
        d_s = pl.op("vector", lambda v: v.tensor_sub(
            tcx[:, :], ta[:, :], tb[:, :]), "dv")
        pl.wait("sync", "dv", d_s)
        sm_prev = pl.dma("sync", "sm", pxs_d[j * P:(j + 1) * P, :],
                         tcx[:, :])
        d_sub_prev = d_s
    pl.wait("sync", "sm", sm_prev)
    pl.dma_pair("sync", "sm", [
        (xsc_d[:, TOK:L], pxs_d, xsc_d[:, 0:TOK], pxs_d),
        (xsc_d[:, 0:TOK], xs_d[0:1024, :], xsc_d[:, TOK:L],
         xs_d[0:1024, :]),
    ])
    es_x.close()
    es_dbc.close()

    if DEBUG:
        dbc_dbg = nc.dram_tensor("dbc_dbg", [DTR, L], F16,
                                 kind="ExternalOutput").ap()
        bc_dbg = nc.dram_tensor("bc_dbg", [32, L], F16,
                                kind="ExternalOutput").ap()
        xs_dbg = nc.dram_tensor("xs_dbg", [DM, TOK], F32,
                                kind="ExternalOutput").ap()
        y_dbg = nc.dram_tensor("y_dbg", [1024, L], F32,
                               kind="ExternalOutput").ap()
        pl.wait("sync", "le", pl.val("le"))
        pl.dma("sync", "sm", dbc_dbg, dbc16[:, :])
        pl.dma("sync", "sm", bc_dbg, bc16[0:32, :])
        pl.dma("sync", "sm", xs_dbg, xs_d)

    es_p4 = contextlib.ExitStack()
    sb4 = lambda n, shp, dt: es_p4.enter_context(nc.sbuf_tensor(n, shp, dt))
    xsb = sb4("xsb", [P, 2 * DTMY * CH], F32)
    u8t = sb4("u8t", [P, DTMY * CH], F16)
    d16 = sb4("d16", [P, DTMY * CH], F16)
    du16 = sb4("du16", [P, DTMY * CH], F16)
    ytile = sb4("ytile", [P, DTMY * CH], F32)
    spr = sb4("spr", [P, 2 * CH], F32)
    spa = sb4("spa", [P, 2 * CH], F32)
    dAs = sb4("dAs", [P, 2 * CH], F32)
    dbu = sb4("dbu", [P, 2 * CH], F32)
    hbuf = sb4("hbuf", [P, 2 * CH], F32)
    hcs = sb4("hcs", [P, 2 * CH], F32)
    bsb = sb4("bsb", [P, 2 * CH], F16)
    csb = sb4("csb", [P, 2 * CH], F16)

    pl.wait("sync", "sm", pl.val("sm"))
    pl.wait("scalar", "dv", d_aneg)
    pl.wait("tensor", "dv", d_db)
    pl.wait("tensor", "lw", l_wdt)
    pl.wait("tensor", "lw", l_oh)
    pl.wait("gpsimd", "dv", d_aneg)
    xsl_done = {}
    y_stored = {}
    H = {"a_dA": {}, "p_dbu": {}, "d_h": {}, "p_hc": {}, "p_hc_flat": {},
         "d_y": {}, "dd": {}}
    d_y_last = 0
    po_chunk_end = {}
    u = -1
    for c in range(SC):
        slot = c % 2
        if c >= 2:
            pl.wait("sync", "ac", xsl_done[c - 2])
        for j in range(DTMY):
            l_xs = pl.dma(
                "sync", "le",
                xsb[:, (slot * DTMY + j) * CH:(slot * DTMY + j + 1) * CH],
                xsc_d[j * P:(j + 1) * P, c * CH:(c + 1) * CH])
        pl.wait("scalar", "le", l_xs)
        if c > 0:
            pl.wait("scalar", "dv", H["dd"][(c - 1, DTMY - 1)])
        a_u = pl.op("scalar", lambda a, slot=slot: a.activation(
            u8t[:, :], xsb[:, slot * DTMY * CH:(slot + 1) * DTMY * CH],
            AF.Silu), "ac")
        xsl_done[c] = a_u
        for j in range(DTMY):
            bk = j % 2
            pl.pe_use(bk)

            def mm_dl(t, j=j, c=c, bk=bk):
                return t.matmul(bank[bk][:, :], wdt16[:, j * P:(j + 1) * P],
                                dbc16[:, c * CH:(c + 1) * CH],
                                start=True, stop=True)
            gd = pl.op("tensor", mm_dl, "peg")
            pl.wait("scalar", "peg", gd)
            ssl = j % 2
            k_sp = c * DTMY + j
            if k_sp >= 2:
                pl.wait("scalar", "dv", H["dd"][(c, j - 2) if j >= 2 else
                                                (c - 1, DTMY + j - 2)])
            ev1 = pl.inc("evg")
            pl.rec("I", "scalar", "evg", 1)
            h_evg3 = pl.h["evg"]

            def sp_act(a, j=j, bk=bk, ssl=ssl):
                a.activation(spr[:, ssl * CH:(ssl + 1) * CH], bank[bk][:, :],
                             AF.Relu, bias=bdt[:, j:j + 1])
                ins = a.activation(spa[:, ssl * CH:(ssl + 1) * CH],
                                   bank[bk][:, :], AF.Abs,
                                   bias=bdt[:, j:j + 1])
                ins.then_inc(h_evg3, 1)
                a.activation(spa[:, ssl * CH:(ssl + 1) * CH],
                             spa[:, ssl * CH:(ssl + 1) * CH], AF.Exp,
                             scale=-1.0)
                return a.activation(spa[:, ssl * CH:(ssl + 1) * CH],
                                    spa[:, ssl * CH:(ssl + 1) * CH], AF.Ln,
                                    bias=1.0)
            pl.op("scalar", sp_act)
            a_sp = fence_act()
            pl.bank_done(ev1, bk)
            pl.wait("vector", "ac", a_sp)
            if c >= 1 and j == 0:
                pl.wait("vector", "po", po_chunk_end[c - 1])

            def mk_d(v, j=j, ssl=ssl):
                v.tensor_tensor(d16[:, j * CH:(j + 1) * CH],
                                spr[:, ssl * CH:(ssl + 1) * CH],
                                spa[:, ssl * CH:(ssl + 1) * CH], OP.add)
                return v.tensor_tensor(
                    du16[:, j * CH:(j + 1) * CH], d16[:, j * CH:(j + 1) * CH],
                    u8t[:, j * CH:(j + 1) * CH], OP.mult)
            H["dd"][(c, j)] = pl.op("vector", mk_d, "dv")
        d_prep = fence_dv()

        if c == 0:
            for s_ in (0, 1):
                pl.wait("scalar", "dv", d_prep)

                def wm_dA(a, s_=s_):
                    return a.activation(dAs[:, s_ * CH:(s_ + 1) * CH],
                                        d16[:, 0:CH], AF.Exp,
                                        scale=aneg[:, 0:1])
                pl.op("scalar", wm_dA)
                wm_a = fence_act()
                pl.wait("gpsimd", "ac", wm_a)

                def wm_dbu(g, s_=s_):
                    return g.tensor_tensor(dbu[:, s_ * CH:(s_ + 1) * CH],
                                           du16[:, 0:CH],
                                           dAs[:, s_ * CH:(s_ + 1) * CH],
                                           OP.mult)
                wm_p = pl.op("gpsimd", wm_dbu, "po")
                pl.wait("vector", "po", wm_p)

                def wm_h(v, s_=s_):
                    v.tensor_tensor_scan(hbuf[:, s_ * CH:(s_ + 1) * CH],
                                         dAs[:, s_ * CH:(s_ + 1) * CH],
                                         dbu[:, s_ * CH:(s_ + 1) * CH],
                                         0.0, OP.mult, OP.add)
                    v.tensor_copy(strow[0:1, 0:64], strow[0:1, 64:128])
                    v.tensor_copy(strow[0:1, 0:64], strow[0:1, 64:128])
                    v.tensor_copy(strow[0:1, 0:64], strow[0:1, 64:128])
                    return v.tensor_copy(hcs[:, s_ * CH:s_ * CH + 64],
                                         hbuf[:, s_ * CH:s_ * CH + 64])
                wm_d = pl.op("vector", wm_h, "dv")
                pl.wait("gpsimd", "dv", wm_d)

                def wm_hc(g, s_=s_):
                    return g.tensor_tensor(hcs[:, s_ * CH:(s_ + 1) * CH],
                                           hbuf[:, s_ * CH:(s_ + 1) * CH],
                                           dbu[:, s_ * CH:(s_ + 1) * CH],
                                           OP.mult)
                pl.op("gpsimd", wm_hc, "po")

        for n in range(NST):
            bb, bcx = 2 + (n % 2) * 2, 3 + (n % 2) * 2
            pl.pe_use(bb, bcx)

            def mm_bc2(t, n=n, c=c, bb=bb, bcx=bcx):
                t.matmul(bank[bb][:, :], oh_sb[:, n * P:(n + 1) * P],
                         bc16[0:32, c * CH:(c + 1) * CH],
                         start=True, stop=True)
                return t.matmul(bank[bcx][:, :],
                                oh_sb[:, (16 + n) * P:(17 + n) * P],
                                bc16[0:32, c * CH:(c + 1) * CH],
                                start=True, stop=True)
            gb = pl.op("tensor", mm_bc2, "peg")
            pl.wait("scalar", "peg", gb)
            if n == 0:
                pl.wait("scalar", "dv", d_prep)
            sbl = n % 2
            un = c * NST + n
            if un >= 2:
                pl.wait("scalar", "po", H["p_hc"][(un - 2, DTMY - 1)])
            def bc_ev(a, bb=bb, bcx=bcx, sbl=sbl):
                a.activation(bsb[:, sbl * CH:(sbl + 1) * CH],
                             bank[bb][:, :], AF.Copy)
                return a.activation(csb[:, sbl * CH:(sbl + 1) * CH],
                                    bank[bcx][:, :], AF.Copy)
            pl.op("scalar", bc_ev)
            ev_bc = fence_act("evg", 2)
            pl.bank_done(ev_bc, bb, bcx)

            for j in range(DTMY):
                u += 1
                ss = u % 2
                col = j * NST + n
                if u >= 2:
                    pl.wait("scalar", "dv", H["d_h"][u - 2])

                def mk_dA(a, j=j, ss=ss, col=col):
                    return a.activation(dAs[:, ss * CH:(ss + 1) * CH],
                                        d16[:, j * CH:(j + 1) * CH], AF.Exp,
                                        scale=aneg[:, col:col + 1])
                pl.op("scalar", mk_dA)
                H["a_dA"][u] = fence_act()
                pl.wait("gpsimd", "ac", H["a_dA"][u])
                pl.wait("gpsimd", "evg", ev_bc)
                if u >= 2:
                    pl.wait("gpsimd", "dv", H["d_h"][u - 2])

                def mk_dbu(g, j=j, ss=ss, sbl=sbl):
                    return g.tensor_tensor(dbu[:, ss * CH:(ss + 1) * CH],
                                           du16[:, j * CH:(j + 1) * CH],
                                           bsb[:, sbl * CH:(sbl + 1) * CH],
                                           OP.mult)
                H["p_dbu"][u] = pl.op("gpsimd", mk_dbu, "po")
                pl.wait("vector", "ac", H["a_dA"][u])
                pl.wait("vector", "po", H["p_dbu"][u])
                if u >= 2:
                    pl.wait("vector", "po", H["p_hc_flat"][u - 2])
                ccol = n * DTMY + j

                def mk_h(v, ss=ss, ccol=ccol):
                    v.tensor_tensor_scan(hbuf[:, ss * CH:(ss + 1) * CH],
                                         dAs[:, ss * CH:(ss + 1) * CH],
                                         dbu[:, ss * CH:(ss + 1) * CH],
                                         carry[:, ccol:ccol + 1],
                                         OP.mult, OP.add)
                    v.tensor_copy(strow[0:1, 0:64], strow[0:1, 64:128])
                    v.tensor_copy(strow[0:1, 0:64], strow[0:1, 64:128])
                    v.tensor_copy(strow[0:1, 0:64], strow[0:1, 64:128])
                    return v.tensor_copy(
                        carry[:, ccol:ccol + 1],
                        hbuf[:, ss * CH + CH - 1:ss * CH + CH])
                H["d_h"][u] = pl.op("vector", mk_h, "dv")
                pl.wait("gpsimd", "dv", H["d_h"][u])
                if u >= 2:
                    pl.wait("gpsimd", "dv", H["d_y"][u - 2])

                def mk_hc(g, ss=ss, sbl=sbl):
                    return g.tensor_tensor(hcs[:, ss * CH:(ss + 1) * CH],
                                           hbuf[:, ss * CH:(ss + 1) * CH],
                                           csb[:, sbl * CH:(sbl + 1) * CH],
                                           OP.mult)
                ph = pl.op("gpsimd", mk_hc, "po")
                H["p_hc"][(un, j)] = ph
                H["p_hc_flat"][u] = ph
                pl.wait("vector", "po", ph)
                if n == 0 and c > 0:
                    pl.wait("vector", "sy", y_stored[c - 1])

                def mk_y(v, n=n, j=j, ss=ss):
                    if n == 0:
                        return v.tensor_copy(ytile[:, j * CH:(j + 1) * CH],
                                             hcs[:, ss * CH:(ss + 1) * CH])
                    return v.tensor_tensor(ytile[:, j * CH:(j + 1) * CH],
                                           ytile[:, j * CH:(j + 1) * CH],
                                           hcs[:, ss * CH:(ss + 1) * CH],
                                           OP.add)
                H["d_y"][u] = pl.op("vector", mk_y, "dv")
        po_chunk_end[c] = pl.val("po")
        d_yf = fence_dv()
        pl.wait("sync", "dv", d_yf)
        for j in range(DTMY):
            y_stored[c] = pl.dma(
                "sync", "sy",
                y_d[j * P:(j + 1) * P, c * CH:(c + 1) * CH],
                ytile[:, j * CH:(j + 1) * CH])

    if DEBUG:
        an_dbg = nc.dram_tensor("an_dbg", [P, DTMY * NST], F32,
                                kind="ExternalOutput").ap()
        wdt_dbg = nc.dram_tensor("wdt_dbg", [DTR, 1024], F16,
                                 kind="ExternalOutput").ap()
        d16_dbg = nc.dram_tensor("d16_dbg", [P, DTMY * CH], F16,
                                 kind="ExternalOutput").ap()
        u8_dbg = nc.dram_tensor("u8_dbg", [P, DTMY * CH], F16,
                                kind="ExternalOutput").ap()
        ca_dbg = nc.dram_tensor("ca_dbg", [P, NST * DTMY], F32,
                                kind="ExternalOutput").ap()
        pl.wait("sync", "sy", pl.val("sy"))
        pl.dma("sync", "sm", ca_dbg, carry[:, :])
        pl.dma("sync", "sm", an_dbg, aneg[:, :])
        pl.dma("sync", "sm", wdt_dbg, wdt16[:, :])
        pl.dma("sync", "sm", d16_dbg, d16[:, :])
        pl.dma("sync", "sm", u8_dbg, u8t[:, :])
        pl.dma("sync", "sm", y_dbg, y_d)
    es_p4.close()
    es_scan.close()

    es_p5 = contextlib.ExitStack()
    sb5 = lambda n, shp, dt: es_p5.enter_context(nc.sbuf_tensor(n, shp, dt))
    gacc = sb5("gacc", [P, KT * TOK], F32)
    es_p5s = contextlib.ExitStack()
    sb5s = lambda n, shp, dt: es_p5s.enter_context(nc.sbuf_tensor(n, shp, dt))
    dlo = sb5s("dlo", [P, 4 * TOK], F16)
    dhi = sb5s("dhi", [P, 4 * TOK], F16)
    dpk = sb5s("dpk", [P, TOK], I16)
    dl16 = sb5s("dl16", [P, TOK], I16)
    dh16 = sb5s("dh16", [P, TOK], I16)
    dsc16 = sb5s("dsc16", [8, TOK], F16)
    intb = sb5s("intb", [P, 2 * 8 * CH], F16)

    dsc_r = down_sc.rearrange("(t r) n -> t r n", r=8)
    int_r = int_d.rearrange("(t p) n -> t p n", p=P)
    pl.wait("sync", "sy", pl.val("sy"))
    dq_done = {-1: 0}
    intb_used = {}
    mmdn_last = {-1: 0}
    ev_gacc = 0
    k_dq = 0
    for gb in range(NG_B):
        for j in range(4):
            td = 4 * gb + j
            pl.wait("gpsimd", "dv", dq_done[k_dq - 1])
            l_dp = pl.dma("gpsimd", "lp", dpk[:, :], down_pk[td])
            l_ds = pl.dma("gpsimd", "ls", dsc16[:, :], dsc_r[td])
            pl.wait("tensor", "ls", l_ds)
            pl.wait("vector", "lp", l_dp)
            if j == 0:
                pl.wait("vector", "peg", mmdn_last[gb - 1])

            def deq_a(v):
                v.tensor_scalar(dl16[:, :], dpk[:, :], 7, 9,
                                OP.bitwise_and, OP.logical_shift_left)
                v.tensor_scalar(dh16[:, :], dpk[:, :], 8, 12,
                                OP.bitwise_and, OP.logical_shift_left)
                return v.tensor_tensor(dl16[:, :], dl16[:, :], dh16[:, :],
                                       OP.bitwise_or)
            d_a = pl.op("vector", deq_a, "dv")
            pl.wait("tensor", "dv", d_a)
            for s in range(4):
                bsc = 6 + s % 2
                pl.pe_use(bsc)

                def mm_ds(t, s=s, bsc=bsc):
                    return t.matmul(bank[bsc][:, :], e8_sb[:, :],
                                    dsc16[:, s * CH:(s + 1) * CH],
                                    start=True, stop=True)
                gq = pl.op("tensor", mm_ds, "peg")
                pl.wait("vector", "peg", gq)
                pl.evac("vector", lambda v, j=j, s=s, bsc=bsc:
                        v.tensor_tensor(
                            dlo[:, j * TOK + s * CH:j * TOK + (s + 1) * CH],
                            dl16[:, s * CH:(s + 1) * CH].bitcast(F16),
                            bank[bsc][:, :], OP.mult), [bsc])

            def deq_b(v):
                v.tensor_scalar(dl16[:, :], dpk[:, :], 112, 5,
                                OP.bitwise_and, OP.logical_shift_left)
                v.tensor_scalar(dh16[:, :], dpk[:, :], 128, 8,
                                OP.bitwise_and, OP.logical_shift_left)
                return v.tensor_tensor(dh16[:, :], dl16[:, :], dh16[:, :],
                                       OP.bitwise_or)
            d_b = pl.op("vector", deq_b, "dv")
            pl.wait("tensor", "dv", d_b)
            for s in range(4):
                bsc = 6 + s % 2
                pl.pe_use(bsc)

                def mm_ds2(t, s=s, bsc=bsc):
                    return t.matmul(bank[bsc][:, :], e8_sb[:, :],
                                    dsc16[:, s * CH:(s + 1) * CH],
                                    start=True, stop=True)
                gq = pl.op("tensor", mm_ds2, "peg")
                pl.wait("vector", "peg", gq)
                pl.evac("vector", lambda v, j=j, s=s, bsc=bsc:
                        v.tensor_tensor(
                            dhi[:, j * TOK + s * CH:j * TOK + (s + 1) * CH],
                            dh16[:, s * CH:(s + 1) * CH].bitcast(F16),
                            bank[bsc][:, :], OP.mult), [bsc])
            h_dvx = pl.h["dv"]
            dq_done[k_dq] = pl.inc("dv")
            pl.rec("I", "vector", "dv", 1)
            pl.emit("vector", lambda v, h=h_dvx: v.tensor_copy(
                dl16[:, 0:CH], dh16[:, 0:CH]).then_inc(h, 1))
            k_dq += 1

        pl.wait("tensor", "dv", dq_done[k_dq - 1])
        for c in range(NCH):
            islot = c % 2
            ic = gb * NCH + c
            if ic >= 2:
                pl.wait("sync", "peg", intb_used[ic - 2])
            for kj in range(8):
                td = 4 * gb + (kj % 4)
                row = td if kj < 4 else 64 + td
                l_int = pl.dma(
                    "sync", "li",
                    intb[:, (islot * 8 + kj) * CH:(islot * 8 + kj + 1) * CH],
                    int_r[row, :, c * CH:(c + 1) * CH])
            pl.wait("tensor", "li", l_int)
            for m in range(KT):
                bk = (m + c * KT) % 4
                pl.pe_use(bk)

                def mm_dn(t, m=m, islot=islot, bk=bk):
                    for kj in range(8):
                        lhs = dlo if kj < 4 else dhi
                        jj = kj % 4
                        ins = t.matmul(
                            bank[bk][:, :],
                            lhs[:, jj * TOK + m * P:jj * TOK + (m + 1) * P],
                            intb[:, (islot * 8 + kj) * CH:
                                 (islot * 8 + kj + 1) * CH],
                            start=(kj == 0), stop=(kj == 7))
                    return ins
                gd = pl.op("tensor", mm_dn, "peg")
                if m == KT - 1:
                    intb_used[ic] = gd
                    if c == NCH - 1:
                        mmdn_last[gb] = gd
                pl.wait("vector", "peg", gd)
                ev_gacc = pl.evac(
                    "vector", lambda v, m=m, c=c, bk=bk, gb=gb:
                    (v.tensor_copy(
                        gacc[:, m * TOK + c * CH:m * TOK + (c + 1) * CH],
                        bank[bk][:, :]) if gb == 0 else
                     v.tensor_tensor(
                        gacc[:, m * TOK + c * CH:m * TOK + (c + 1) * CH],
                        gacc[:, m * TOK + c * CH:m * TOK + (c + 1) * CH],
                        bank[bk][:, :], OP.add)), [bk])

    ev_gaccf = fence_dv()
    es_p5s.close()
    gstg = sb5("gstg", [P, 2 * TOK], F16)
    gs_used = {}
    pl.wait("scalar", "dv", ev_gaccf)
    for m in range(KT):
        gsl = m % 2
        if m >= 2:
            pl.wait("scalar", "sm", gs_used[m - 2])
        pl.op("scalar", lambda a, m=m, gsl=gsl: a.activation(
            gstg[:, gsl * TOK:(gsl + 1) * TOK],
            gacc[:, m * TOK:(m + 1) * TOK], AF.Silu))
        a_g = fence_act()
        pl.wait("sync", "ac", a_g)
        gs_used[m] = pl.dma("sync", "sm", g_d[m * P:(m + 1) * P, :],
                            gstg[:, gsl * TOK:(gsl + 1) * TOK])
    pl.wait("gpsimd", "sy", pl.val("sy"))
    pl.wait("gpsimd", "sm", pl.val("sm"))
    pl.wait("gpsimd", "peg", pl.val("peg"))
    pl.wait("gpsimd", "dv", pl.val("dv"))
    pl.wait("gpsimd", "ac", pl.val("ac"))
    c_cc3 = pl.op("gpsimd", lambda g: g.collective_compute(
        "AllReduce", OP.add, replica_groups=RG, ins=[y_d], outs=[cc3o]),
        "cc")
    for _e in Plan.ENG:
        pl.wait(_e, "cc", c_cc3)
    es_p5.close()

    es_p6 = contextlib.ExitStack()
    sb6 = lambda n, shp, dt: es_p6.enter_context(nc.sbuf_tensor(n, shp, dt))
    yall = sb6("yall", [P, KT * CH], F32)
    xsl = sb6("xsl", [P, KT * CH], F32)
    rl = sb6("rl", [P, KT * CH], F16)
    gl = sb6("gl", [P, KT * CH], F16)
    xl = sb6("xl", [P, KT * CH], F32)
    pys = sb6("pys", [P, CH], F32)
    uep = sb6("uep", [P, 2 * CH], F32)
    srb = sb6("srb", [P, 2 * CH], F16)
    op16 = sb6("op16", [P, KT * CH], F16)
    ostg = sb6("ostg", [P, 2 * CH], F32)
    epsc = sb6("epsc", [P, 2 * CH], F32)
    wol = sb6("wol", [P, 2 * KT * P], F16)

    pl.wait("sync", "cc", c_cc3)
    pl.wait("sync", "sm", pl.val("sm"))
    pl.wait("gpsimd", "sm", pl.val("sm"))
    pl.wait("gpsimd", "peg", mmdn_last[NG_B - 1])
    pl.dma_pair("sync", "le", [
        (ymg_d[0:1024, :], y_d[:, 0:TOK], ymg_d[0:1024, :], y_d[:, TOK:L]),
        (ymg_d[1024:DM, :], cc3o[:, 0:TOK], ymg_d[1024:DM, :],
         cc3o[:, TOK:L]),
    ])
    pl.wait("sync", "le", pl.val("le"))
    r_r = r_d.rearrange("(t p) n -> t p n", p=P)
    g_r = g_d.rearrange("(t p) n -> t p n", p=P)
    wol_r = wol.ap().rearrange("p (t n) -> p t n", n=P)
    ost_used = {}
    wol_used = {}
    d_f_last = 0
    a_ep_last = 0
    d_op_hist = {}
    for c in range(NCH):
        if c > 0:
            pl.wait("sync", "dv", d_f_last)
            pl.wait("sync", "ac", a_ep_last)
        for j in range(DTMY):
            pl.dma("sync", "le", yall[:, j * CH:(j + 1) * CH],
                   ymg_d[j * P:(j + 1) * P, c * CH:(c + 1) * CH])
        d_py = 0
        for j in range(DTMY):
            if d_py:
                pl.wait("sync", "dv", d_py)
            lp2 = pl.dma("sync", "le", pys[:, :],
                         ymg_d[1024 + j * P:1024 + (j + 1) * P,
                               c * CH:(c + 1) * CH])
            pl.wait("vector", "le", lp2)
            d_py = pl.op("vector", lambda v, j=j: v.tensor_sub(
                yall[:, (8 + j) * CH:(9 + j) * CH], pys[:, :],
                yall[:, j * CH:(j + 1) * CH]), "dv")
        l_xs = pl.dma("sync", "le",
                      xsl.ap().rearrange("p (t n) -> p t n", n=CH),
                      xs_r[:, :, c * CH:(c + 1) * CH].transpose([1, 0, 2]))
        l_r = pl.dma("sync", "le",
                     rl.ap().rearrange("p (t n) -> p t n", n=CH),
                     r_r[:, :, c * CH:(c + 1) * CH].transpose([1, 0, 2]))
        l_g = pl.dma("sync", "le",
                     gl.ap().rearrange("p (t n) -> p t n", n=CH),
                     g_r[:, :, c * CH:(c + 1) * CH].transpose([1, 0, 2]))
        l_x = pl.dma("sync", "le",
                     xl.ap().rearrange("p (t n) -> p t n", n=CH),
                     x_r[:, :, c * CH:(c + 1) * CH].transpose([1, 0, 2]))
        d_py = fence_dv()
        pl.wait("scalar", "le", l_x)
        pl.wait("vector", "le", l_x)
        pl.wait("vector", "dv", d_py)
        for j in range(KT):
            ssl = j % 2
            if j >= 2:
                pl.wait("scalar", "dv", d_op_hist[j - 2])

            def silus(a, j=j, ssl=ssl):
                a.activation(uep[:, ssl * CH:(ssl + 1) * CH],
                             xsl[:, j * CH:(j + 1) * CH], AF.Silu)
                return a.activation(srb[:, ssl * CH:(ssl + 1) * CH],
                                    rl[:, j * CH:(j + 1) * CH], AF.Silu)
            pl.op("scalar", silus)
            a_s = fence_act()
            a_ep_last = a_s
            pl.wait("vector", "ac", a_s)

            def mk_op(v, j=j, ssl=ssl):
                v.scalar_tensor_tensor(
                    epsc[:, ssl * CH:(ssl + 1) * CH],
                    uep[:, ssl * CH:(ssl + 1) * CH], dvec[:, j:j + 1],
                    yall[:, j * CH:(j + 1) * CH], OP.mult, OP.add)
                return v.tensor_tensor(op16[:, j * CH:(j + 1) * CH],
                                       epsc[:, ssl * CH:(ssl + 1) * CH],
                                       srb[:, ssl * CH:(ssl + 1) * CH],
                                       OP.mult)
            d_op_hist[j] = pl.op("vector", mk_op, "dv")
        d_op = fence_dv()
        for m in range(KT):
            wsl = m % 2
            key = c * KT + m
            if key >= 2:
                pl.wait("gpsimd", "peg", wol_used[key - 2])
            l_w = pl.dma("gpsimd", "lw",
                         wol_r[:, wsl * KT:(wsl + 1) * KT, :], wout_t[m])
            pl.wait("tensor", "lw", l_w)
            if m == 0:
                pl.wait("tensor", "dv", d_op)
            bk = key % 2
            pl.pe_use(bk)

            def mm_out(t, wsl=wsl, bk=bk):
                for kt in range(KT):
                    ins = t.matmul(
                        bank[bk][:, :],
                        wol[:, (wsl * KT + kt) * P:(wsl * KT + kt + 1) * P],
                        op16[:, kt * CH:(kt + 1) * CH],
                        start=(kt == 0), stop=(kt == KT - 1))
                return ins
            gw = pl.op("tensor", mm_out, "peg")
            wol_used[key] = gw
            pl.wait("vector", "peg", gw)
            osl = key % 2
            if key >= 2:
                pl.wait("vector", "so", ost_used[key - 2])
            ev1 = pl.inc("evg")
            pl.rec("I", "vector", "evg", 1)
            h_evg5 = pl.h["evg"]

            def fin(v, m=m, bk=bk, osl=osl):
                ins = v.tensor_tensor(epsc[:, osl * CH:(osl + 1) * CH],
                                      bank[bk][:, :],
                                      xsl[:, m * CH:(m + 1) * CH], OP.add)
                ins.then_inc(h_evg5, 1)
                v.tensor_tensor(epsc[:, osl * CH:(osl + 1) * CH],
                                epsc[:, osl * CH:(osl + 1) * CH],
                                gl[:, m * CH:(m + 1) * CH], OP.mult)
                return v.tensor_tensor(ostg[:, osl * CH:(osl + 1) * CH],
                                       epsc[:, osl * CH:(osl + 1) * CH],
                                       xl[:, m * CH:(m + 1) * CH], OP.add)
            pl.op("vector", fin, "dv")
            d_f = fence_dv()
            d_f_last = d_f
            pl.bank_done(ev1, bk)
            pl.wait("sync", "dv", d_f)
            ost_used[key] = pl.dma(
                "sync", "so",
                out_t[m * P:(m + 1) * P, c * CH:(c + 1) * CH],
                ostg[:, osl * CH:(osl + 1) * CH])

    pl.wait("sync", "so", pl.val("so"))
    es_p6.close()

    pl.lag_waits()
    pl.simulate()

    with es:
        with nc.Block() as block:
            @block.sync
            def _(eng):
                for fn in pl.items["sync"]:
                    fn(eng)

            @block.vector
            def _(eng):
                for fn in pl.items["vector"]:
                    fn(eng)

            @block.scalar
            def _(eng):
                for fn in pl.items["scalar"]:
                    fn(eng)

            @block.tensor
            def _(eng):
                for fn in pl.items["tensor"]:
                    fn(eng)

            @block.gpsimd
            def _(eng):
                for fn in pl.items["gpsimd"]:
                    fn(eng)
    return nc



_NC_CACHE = [None]


def _run(inputs):
    in_maps, perms = _stage(inputs)
    if _NC_CACHE[0] is None:
        _NC_CACHE[0] = build_program()
    nc = _NC_CACHE[0]
    res = run_bass_kernel_spmd(nc, in_maps, core_ids=list(range(8)))
    return res, perms


def kernel(**inputs):
    res, perms = _run(inputs)
    out = np.empty((B_, L, DM), np.float32)
    for c in range(8):
        b, par = c // 2, c % 2
        o = np.asarray(res.results[c]["out_t"], np.float32)
        out[b, par * TOK:(par + 1) * TOK][:, perms[par]] = o.T
    return out
